# version 05
# baseline (speedup 1.0000x reference)
import math
import numpy as np
import ml_dtypes

import concourse.bass as bass
import concourse.mybir as mybir
from concourse.bass_utils import run_bass_kernel_spmd
from contextlib import ExitStack

F32 = mybir.dt.float32
BF16 = mybir.dt.bfloat16
AF = mybir.ActivationFunctionType
ALU = mybir.AluOpType
bfloat16 = ml_dtypes.bfloat16

D = 1024
NH = 16
HD = 64
EXT = 2560
NT = 20
QT0 = 3
NQT = 17
OUT_T = 2048
BEXP = 16.0
MASKV = -30000.0
EPS = 1e-5

ENGS = ("sync", "tensor", "vector", "scalar", "gpsimd")


class Sched:

    def __init__(self):
        self.cnt = {}
        self.ops = {e: [] for e in ENGS}
        self.trace = {e: [] for e in ENGS}
        self.label = None

    def w(self, eng, sem, val):
        if val > 0:
            self.ops[eng].append(("w", sem, val))
            self.trace[eng].append(("w", sem, val, self.label, frozenset()))

    def e(self, eng, fn, incs=(), dma=False, banks=()):
        out = {}
        amt = 16 if dma else 1
        incl = []
        for sem in incs:
            self.cnt[sem] = self.cnt.get(sem, 0) + amt
            incl.append((sem, amt))
            out[sem] = self.cnt[sem]
        self.ops[eng].append(("e", fn, incl))
        self.trace[eng].append(("e", incl, self.label, frozenset(banks)))
        return out

    def c(self, sem):
        return self.cnt.get(sem, 0)

    def chain(self, eng, fns, incs=(), banks=()):
        sem = {"vector": "sDVE", "scalar": "sACT", "tensor": "sPE",
               "gpsimd": "sPOOL"}[eng]

        def wrapper(eng_obj, _fns=tuple(fns)):
            r = None
            for i, fn in enumerate(_fns):
                if i:
                    eng_obj.drain()
                r = fn(eng_obj)
            return r
        want = set(incs) | {sem}
        return self.e(eng, wrapper, tuple(sorted(want)), banks=banks)


def build_nc(nt=NT, do_attn=True, do_p2=True, upto=99):
    nc = bass.Bass("TRN2", target_bir_lowering=False)

    d_x = nc.dram_tensor("x_ext", [EXT, D], F32, kind="ExternalInput").ap()
    d_wqkv = nc.dram_tensor("wqkv", [D, 3 * D], F32, kind="ExternalInput").ap()
    d_wout = nc.dram_tensor("wout", [64, 16 * 1024], F32, kind="ExternalInput").ap()
    d_wr = nc.dram_tensor("wr", [D, D], F32, kind="ExternalInput").ap()
    d_wk = nc.dram_tensor("wk", [D, D], F32, kind="ExternalInput").ap()
    d_wv = nc.dram_tensor("wv", [D, D], F32, kind="ExternalInput").ap()
    d_tabs = nc.dram_tensor("tabs", [128, EXT], F32, kind="ExternalInput").ap()
    d_mus = nc.dram_tensor("mus", [128, 16], F32, kind="ExternalInput").ap()
    d_consts = nc.dram_tensor("consts", [128, 8], F32, kind="ExternalInput").ap()
    d_ident = nc.dram_tensor("ident", [128, 128], BF16, kind="ExternalInput").ap()
    d_mfirst = nc.dram_tensor("mfirst", [128, 128], BF16, kind="ExternalInput").ap()
    d_mlast = nc.dram_tensor("mlast", [128, 128], BF16, kind="ExternalInput").ap()
    d_mq = nc.dram_tensor("mq", [128, 2048], BF16, kind="ExternalInput").ap()
    d_out = nc.dram_tensor("out", [OUT_T, D], F32, kind="ExternalOutput").ap()
    d_spill = nc.dram_tensor("x2spill", [17 * 128, D], F32, kind="Internal").ap()

    es = ExitStack()

    def sb(name, shape, dt):
        return es.enter_context(nc.sbuf_tensor(name, shape, dt))

    consts = sb("sb_consts", [128, 8], F32)
    mus = sb("sb_mus", [128, 16], F32)
    ident = sb("sb_ident", [128, 128], BF16)

    sems = {}
    for s in ("sSET", "sWA", "sWB", "sXA", "sXB", "sSPA", "sSPB", "sX2",
              "sSTA", "sSTB", "sPE", "sACT", "sDVE", "sPOOL"):
        sems[s] = es.enter_context(nc.semaphore(s))

    sem_nums = sorted(h.num for h in sems.values())
    sem_rng = range(sem_nums[0], sem_nums[-1] + 1)
    assert len(sem_nums) == len(sem_rng), "semaphore numbers not contiguous"
    nc.all_engine_barrier()
    nc.gpsimd.dma_reset(sem_rng)
    nc.gpsimd.sem_clear(sem_rng)
    nc.all_engine_barrier()

    S = Sched()
    M = {}

    CB = consts[:, 0:1]
    CS = consts[:, 1:2]
    CE = consts[:, 2:3]
    CH = consts[:, 3:4]
    CN = consts[:, 4:5]

    with es:
        es1 = ExitStack()

        def sb1(name, shape, dt):
            return es1.enter_context(nc.sbuf_tensor(name, shape, dt))

        wqkv_b = sb1("wqkv_b", [128, 8 * 3072], BF16)
        wout_b = sb1("wout_b", [64, 16 * 1024], BF16)
        stag = sb1("stag", [128, 2 * 1024], F32)
        kT_ring = sb1("kT_ring", [128, 8 * 1024], BF16)
        v_ring = sb1("v_ring", [128, 8 * 1024], BF16)
        qTp = sb1("qTp", [128, 2 * 2048], BF16)
        xb = sb1("xb", [128, 2 * 1024], F32)
        xln1 = sb1("xln1", [128, 1024], BF16)
        xlt = sb1("xlt", [128, 1024], BF16)
        tabs = sb1("sb_tabs", [128, 2560], F32)
        mfirst = sb1("sb_mfirst", [128, 128], BF16)
        mlast = sb1("sb_mlast", [128, 128], BF16)
        mq = sb1("sb_mq", [128, 2048], BF16)
        tmpA = sb1("tmpA", [128, 2048], F32)
        tmpB = sb1("tmpB", [128, 2048], F32)
        qnat = sb1("qnat", [128, 1024], BF16)
        knat = sb1("knat", [128, 1024], BF16)
        Pb = sb1("Pb", [128, 2 * 1280], BF16)
        PTb = sb1("PTb", [128, 2 * 1280], BF16)
        oT = sb1("oT", [128, 2048], BF16)
        sums = sb1("sums", [128, 128], F32)
        t1b = sb1("t1b", [128, 1024], F32)
        x2s = sb1("x2s", [128, 2 * 1024], F32)
        junk = sb1("junk", [128, 1024], BF16)
        st1 = sb1("st1", [128, 32], F32)
        st2 = sb1("st2", [128, 16], F32)

        PS = es1.enter_context(nc.psum_tensor("PS", [128, 4096], F32))
        PSf = PS.ap()
        PSb = PS.ap().bitcast(BF16)

        setup_ld = None
        for t, h in ((consts, d_consts), (mus, d_mus), (ident, d_ident),
                     (tabs, d_tabs), (mfirst, d_mfirst), (mlast, d_mlast), (mq, d_mq)):
            setup_ld = S.e("sync", (lambda tt, hh: (lambda eng: eng.dma_start(out=tt[:], in_=hh)))(t, h),
                           ("sSET",), dma=True)
        for eng in ("tensor", "vector", "scalar", "gpsimd"):
            S.w(eng, "sSET", setup_ld["sSET"])
        S.e("vector", lambda eng: eng.memset(qTp[:], 0.0), ("sDVE",))

        witems = []
        for c in range(8):
            for hh in range(3):
                witems.append((d_wqkv[c * 128:(c + 1) * 128, hh * 1024:(hh + 1) * 1024],
                               wqkv_b[:, c * 3072 + hh * 1024:c * 3072 + (hh + 1) * 1024], 128))
        for c in range(16):
            witems.append((d_wout[:, c * 1024:(c + 1) * 1024],
                           wout_b[0:64, c * 1024:(c + 1) * 1024], 64))
        for i, (src, dst, nparts) in enumerate(witems):
            slot = stag[0:nparts, (i % 2) * 1024:(i % 2) * 1024 + 1024]
            wsem = "sWA" if i % 2 == 0 else "sWB"
            if i >= 2:
                S.w("sync", "sDVE", M[("cast", i - 2)]["sDVE"])
            M[("wld", i)] = S.e("sync",
                                (lambda ss, hh: (lambda eng: eng.dma_start(out=ss, in_=hh)))(slot, src),
                                (wsem,), dma=True)
            S.w("vector", wsem, M[("wld", i)][wsem])
            M[("cast", i)] = S.e("vector",
                                 (lambda ss, dd: (lambda eng: eng.tensor_copy(dd, ss)))(slot, dst),
                                 ("sDVE",))

        def x_tile(j):
            return xb[:, (j % 2) * 1024:(j % 2) * 1024 + 1024]

        def st1c(j, k):
            base = (j % 2) * 8
            return st1[:, base + k:base + k + 1]

        def emit_phase1():
            for j in range(nt):
                qt = j if j >= QT0 else None
                rot = j % 2

                if j >= 2:
                    if ("x2", j - 2) in M:
                        S.w("sync", "sPOOL", M[("x2", j - 2)]["sPOOL"])
                    else:
                        S.w("sync", "sDVE", M[("c1", j - 2)]["sDVE"])
                        S.w("sync", "sACT", M[("ssq", j - 2)]["sACT"])
                xsem = "sXA" if j % 2 == 0 else "sXB"
                M[("xld", j)] = S.e("sync",
                                    (lambda jj: (lambda eng: eng.dma_start(out=x_tile(jj), in_=d_x[jj * 128:(jj + 1) * 128, :])))(j),
                                    (xsem,), dma=True)

                S.w("scalar", xsem, M[("xld", j)][xsem])
                M[("ssq", j)] = S.e("scalar",
                                    (lambda jj: (lambda eng: eng.activation(junk[:], x_tile(jj), AF.Square,
                                                                            accum_out=st1c(jj, 1))))(j),
                                    ("sACT",))
                S.w("vector", xsem, M[("xld", j)][xsem])
                pre = S.chain("vector", [
                    (lambda jj: (lambda eng: eng.reduce_sum(st1c(jj, 0), x_tile(jj), axis=mybir.AxisListType.X)))(j),
                    (lambda jj: (lambda eng: eng.tensor_scalar(st1c(jj, 2), st1c(jj, 0), 1.0 / 1024.0, None, ALU.mult)))(j),
                    (lambda jj: (lambda eng: eng.tensor_tensor(st1c(jj, 3), st1c(jj, 2), st1c(jj, 2), ALU.mult)))(j),
                ])
                S.w("vector", "sACT", M[("ssq", j)]["sACT"])
                S.w("vector", "sDVE", pre["sDVE"])
                M[("var", j)] = S.e("vector",
                                    (lambda jj: (lambda eng: eng.scalar_tensor_tensor(st1c(jj, 4), st1c(jj, 1), 1.0 / 1024.0,
                                                                                      st1c(jj, 3), ALU.mult, ALU.subtract)))(j),
                                    ("sDVE",))
                S.w("scalar", "sDVE", M[("var", j)]["sDVE"])
                M[("rs1", j)] = S.chain("scalar", [
                    (lambda jj: (lambda eng: eng.activation(st1c(jj, 5), st1c(jj, 4), AF.Ln, bias=CE, scale=1.0)))(j),
                    (lambda jj: (lambda eng: eng.activation(st1c(jj, 6), st1c(jj, 5), AF.Exp, bias=0.0, scale=CH)))(j),
                ])
                S.w("vector", "sACT", M[("rs1", j)]["sACT"])
                M[("c1", j)] = S.chain("vector", [
                    (lambda jj: (lambda eng: eng.scalar_tensor_tensor(st1c(jj, 7), st1c(jj, 2), -1.0, st1c(jj, 6),
                                                                      ALU.mult, ALU.mult)))(j),
                    (lambda jj: (lambda eng: eng.tensor_scalar(xln1[:], x_tile(jj), st1c(jj, 6), st1c(jj, 7),
                                                               ALU.mult, ALU.add)))(j),
                ])

                S.w("tensor", "sDVE", M[("c1", j)]["sDVE"])
                if j > 0:
                    S.w("tensor", "sDVE", M[("ktev", j - 1)]["sDVE"])
                if ("otev", j - 1, 7) in M:
                    S.w("tensor", "sDVE", M[("otev", j - 1, 7)]["sDVE"])

                def t1_fn(jj):
                    def f(eng):
                        r = None
                        for c in range(8):
                            r = eng.transpose(PSb[:, 6144 + c * 128:6272 + c * 128],
                                              xln1[:, c * 128:(c + 1) * 128], ident[:])
                        return r
                    return f
                M[("pt1", j)] = S.e("tensor", t1_fn(j), ("sPE",), banks={6})

                S.w("vector", "sPE", M[("pt1", j)]["sPE"])
                M[("e1", j)] = S.e("vector", lambda eng: eng.tensor_copy(xlt[:], PSb[:, 6144:7168]), ("sDVE",), banks={6})

                S.w("tensor", "sDVE", M[("e1", j)]["sDVE"])
                if j == 0:
                    S.w("tensor", "sDVE", M[("cast", 39)]["sDVE"])
                if ("t1", j - 1) in M:
                    S.w("tensor", "sACT", M[("t1", j - 1)]["sACT"])
                    S.w("tensor", "sDVE", M[("nb2", j - 1)]["sDVE"])
                elif j > 0:
                    S.w("tensor", "sACT", M[("vev", j - 1)]["sACT"])
                    if ("ropeK", j - 1, "d") in M:
                        S.w("tensor", "sDVE", M[("ropeK", j - 1, "d")]["sDVE"])

                def qkv_fn(jj):
                    def f(eng):
                        r = None
                        for c in range(8):
                            lhs = xlt[:, c * 128:(c + 1) * 128]
                            for half in range(2):
                                for w in range(3):
                                    r = eng.matmul(PSf[:, w * 1024 + half * 512: w * 1024 + half * 512 + 512],
                                                   lhs,
                                                   wqkv_b[:, c * 3072 + w * 1024 + half * 512: c * 3072 + w * 1024 + half * 512 + 512],
                                                   start=(c == 0), stop=(c == 7))
                        return r
                    return f
                M[("qkv", j)] = S.e("tensor", qkv_fn(j), ("sPE",), banks={0, 1, 2, 3, 4, 5})

                S.w("scalar", "sPE", M[("qkv", j)]["sPE"])
                M[("vev", j)] = S.e("scalar",
                                    (lambda jj: (lambda eng: eng.activation(v_ring[:, (jj % 8) * 1024:(jj % 8) * 1024 + 1024],
                                                                            PSf[:, 2048:3072], AF.Copy)))(j),
                                    ("sACT",), banks={4, 5})

                def rope_fn(jj, base_off, toff):
                    cos_ap = bass.AP(tabs, jj * 128, [[2560, 128], [0, 16], [0, 2], [1, 32]])
                    sin_ap = bass.AP(tabs, jj * 128 + 64, [[2560, 128], [0, 16], [32, 2], [1, 32]])
                    plain = bass.AP(PS, base_off, [[4096, 128], [64, 16], [32, 2], [1, 32]])
                    swap = bass.AP(PS, base_off + 32, [[4096, 128], [64, 16], [-32, 2], [1, 32]])
                    tA = bass.AP(tmpA, toff, [[2048, 128], [64, 16], [32, 2], [1, 32]])
                    tB = bass.AP(tmpB, toff, [[2048, 128], [64, 16], [32, 2], [1, 32]])

                    def f(eng):
                        eng.tensor_tensor(tA, plain, cos_ap, ALU.mult)
                        return eng.tensor_tensor(tB, swap, sin_ap, ALU.mult)
                    return f

                S.w("vector", "sPE", M[("qkv", j)]["sPE"])
                if upto == 21:
                    M[("ropeK", j, "g")] = M[("qkv", j)]
                    M[("ktev", j)] = M[("c1", j)]
                    M[("qtev", j)] = M[("c1", j)]
                if j >= QT0 and upto != 21:
                    if j > QT0:
                        S.w("vector", "sPOOL", M[("ropeQ", j - 1, "g")]["sPOOL"])
                    M[("ropeQ", j, "d")] = S.e("vector", rope_fn(j, 0, 0), ("sDVE",), banks={0, 1})
                    S.w("gpsimd", "sDVE", M[("ropeQ", j, "d")]["sDVE"])
                    M[("ropeQ", j, "g")] = S.e("gpsimd",
                                               lambda eng: eng.tensor_tensor(qnat[:], tmpA[:, 0:1024], tmpB[:, 0:1024], ALU.add),
                                               ("sPOOL",))

                if upto != 21:
                    if j > 0 and ("ropeK", j - 1, "g") in M:
                        S.w("vector", "sPOOL", M[("ropeK", j - 1, "g")]["sPOOL"])
                    M[("ropeK", j, "d")] = S.e("vector", rope_fn(j, 1024, 1024), ("sDVE",), banks={2, 3})
                    S.w("gpsimd", "sDVE", M[("ropeK", j, "d")]["sDVE"])
                    M[("ropeK", j, "g")] = S.e("gpsimd",
                                               lambda eng: eng.tensor_tensor(knat[:], tmpA[:, 1024:2048], tmpB[:, 1024:2048], ALU.add),
                                               ("sPOOL",))

                if j >= QT0 and upto != 21:
                    S.w("tensor", "sPOOL", M[("ropeQ", j, "g")]["sPOOL"])
                    if j > QT0:
                        S.w("tensor", "sDVE", M[("qtev", j - 1)]["sDVE"])

                    def tq_fn(jj):
                        def f(eng):
                            r = None
                            for c in range(8):
                                r = eng.transpose(PSb[:, 7168 + c * 128:7296 + c * 128],
                                                  qnat[:, c * 128:(c + 1) * 128], ident[:])
                            return r
                        return f
                    M[("ptq", j)] = S.e("tensor", tq_fn(j), ("sPE",), banks={7})
                    S.w("vector", "sPE", M[("ptq", j)]["sPE"])
                    def qtev_fn(jj):
                        rot = jj % 2
                        def f(eng):
                            srcA = bass.AP(PS, 3584, [[4096, 64], [64, 8], [1, 64]]).bitcast(BF16) if False else None
                            sA = bass.AP(PSb.tensor if hasattr(PSb, "tensor") else PS, 0, [])
                            a_src = bass.AP(PS, 3584, [[2048, 64], [64, 8], [1, 64]])
                            return eng.tensor_copy(
                                bass.AP(qTp, rot * 2048, [[4096, 64], [256, 8], [1, 128]]),
                                a_src.bitcast(BF16))
                        return f
                    def qtev2_fn(jj):
                        rot = jj % 2
                        def f(eng):
                            eng.tensor_copy(
                                bass.AP(qTp, rot * 2048, [[4096, 64], [256, 8], [1, 128]]),
                                bass.AP(qTp, 0, []) if False else
                                PSb[0:64, 7168:8192].rearrange("p (c f) -> p c f", c=8))
                            return eng.tensor_copy(
                                bass.AP(qTp, 4096 * 64 + rot * 2048 + 128, [[4096, 64], [256, 8], [1, 128]]),
                                PSb[64:128, 7168:8192].rearrange("p (c f) -> p c f", c=8))
                        return f
                    M[("qtev", j)] = S.e("vector", qtev2_fn(j), ("sDVE",), banks={7})

                if upto != 21:
                    S.w("tensor", "sPOOL", M[("ropeK", j, "g")]["sPOOL"])
                S.w("tensor", "sDVE", M[("e1", j)]["sDVE"])

                if upto != 21:
                    def tk_fn(jj):
                        def f(eng):
                            r = None
                            for c in range(8):
                                r = eng.transpose(PSb[:, 6144 + c * 128:6272 + c * 128],
                                                  knat[:, c * 128:(c + 1) * 128], ident[:])
                            return r
                        return f
                    M[("ptk", j)] = S.e("tensor", tk_fn(j), ("sPE",), banks={6})
                    S.w("vector", "sPE", M[("ptk", j)]["sPE"])
                    M[("ktev", j)] = S.e("vector",
                                         (lambda jj: (lambda eng: eng.tensor_copy(kT_ring[:, (jj % 8) * 1024:(jj % 8) * 1024 + 1024],
                                                                                  PSb[:, 6144:7168])))(j),
                                         ("sDVE",), banks={6})

                if qt is not None and do_attn:
                    emit_attn(qt, upto)

        def emit_attn(qt, upto=99):
            j = qt
            kts = list(range(max(0, qt - 4), qt))
            nkt = len(kts)
            qclass = qt - 4 if 4 <= qt <= 7 else None

            for hp in range(8):
                if False:
                    for ss in ("sDVE", "sACT", "sPOOL"):
                        S.w("tensor", ss, S.c(ss))
                    for ss in ("sPE", "sACT"):
                        S.w("vector", ss, S.c(ss))
                    for ss in ("sPE", "sDVE"):
                        S.w("scalar", ss, S.c(ss))
                rot = hp % 2
                soff = rot * 1536
                S.w("tensor", "sDVE", M[("qtev", qt)]["sDVE"])
                S.w("tensor", "sDVE", M[("ktev", qt)]["sDVE"])
                if hp == 1:
                    S.w("tensor", "sACT", M[("vev", qt)]["sACT"])
                for key in ((qt, hp - 2) if hp >= 2 else (qt - 1, 6 + hp),):
                    for stage, sem in (("exp", "sACT"), ("ptev", "sDVE"), ("nm", "sDVE"),
                                       ("mk", "sDVE")):
                        if (stage, key[0], key[1]) in M:
                            S.w("tensor", sem, M[(stage, key[0], key[1])][sem])
                            if stage in ("exp",):
                                continue
                            break
                    else:
                        continue

                def sc_fn(qtt, hpp, so):
                    def f(eng):
                        r = None
                        for hh in range(2):
                            h = 2 * hpp + hh
                            lhs = qTp[:, (qtt % 2) * 2048 + h * 128:(qtt % 2) * 2048 + (h + 1) * 128]
                            hoff = so + hh * 640
                            for ki, ka in enumerate(kts):
                                slot = ka % 8
                                rhs = kT_ring[:, slot * 1024 + hpp * 128: slot * 1024 + hpp * 128 + 128]
                                r = eng.matmul(PSf[:, hoff + ki * 128: hoff + ki * 128 + 128],
                                               lhs, rhs, start=True, stop=True)
                            slot = qtt % 8
                            rhs = kT_ring[:, slot * 1024 + hpp * 128: slot * 1024 + hpp * 128 + 128]
                            r = eng.matmul(PSf[:, hoff + 512: hoff + 640],
                                           lhs, rhs, start=True, stop=True)
                        return r
                    return f
                M[("sc", qt, hp)] = S.e("tensor", sc_fn(qt, hp, soff), ("sPE",), banks={3 * rot, 3 * rot + 1, 3 * rot + 2})

                if upto in (13, 15) and hp > 0:
                    break
                if upto < 2 or upto in (11, 13, 15):
                    continue
                S.w("vector", "sPE", M[("sc", qt, hp)]["sPE"])

                def mk_fn(qtt, hpp, so):
                    def f(eng):
                        if upto == 12:
                            return eng.tensor_copy(junk[:, 0:128], xln1[:, 0:128])
                        r = None
                        if qclass is not None:
                            src = bass.AP(PS, so, [[4096, 128], [640, 2], [1, 512]])
                            m = bass.AP(mq, qclass * 512, [[2048, 128], [0, 2], [1, 512]])
                            r = eng.tensor_tensor(src, src, m, ALU.add)
                        elif qtt >= 8:
                            src = bass.AP(PS, so, [[4096, 128], [640, 2], [1, 128]])
                            m = bass.AP(mfirst, 0, [[128, 128], [0, 2], [1, 128]])
                            r = eng.tensor_tensor(src, src, m, ALU.add)
                        srcd = bass.AP(PS, so + 512, [[4096, 128], [640, 2], [1, 128]])
                        md = bass.AP(mlast, 0, [[128, 128], [0, 2], [1, 128]])
                        return eng.tensor_tensor(srcd, srcd, md, ALU.add)
                    return f
                M[("mk", qt, hp)] = S.e("vector", mk_fn(qt, hp, soff), ("sDVE",), banks={3 * rot, 3 * rot + 1, 3 * rot + 2})

                if upto < 3:
                    continue
                S.w("scalar", "sDVE", M[("mk", qt, hp)]["sDVE"])

                def exp_fn(qtt, hpp, so):
                    srot = (qtt % 2) * 64
                    def f(eng):
                        r = None
                        for hh in range(2):
                            h = 2 * hpp + hh
                            hoff = so + hh * 640
                            poff = (hpp % 2) * 1280 + hh * 640
                            if qtt == QT0:
                                eng.activation(Pb[:, poff:poff + 384], PSf[:, hoff:hoff + 384],
                                               AF.Exp, bias=CB, scale=CS,
                                               accum_out=sums[:, srot + h:srot + h + 1])
                                r = eng.activation(Pb[:, poff + 512:poff + 640], PSf[:, hoff + 512:hoff + 640],
                                                   AF.Exp, bias=CB, scale=CS,
                                                   accum_out=sums[:, srot + 16 + h:srot + 16 + h + 1])
                            else:
                                r = eng.activation(Pb[:, poff:poff + 640], PSf[:, hoff:hoff + 640],
                                                   AF.Exp, bias=CB, scale=CS,
                                                   accum_out=sums[:, srot + h:srot + h + 1])
                        return r
                    return f
                M[("exp", qt, hp)] = S.e("scalar", exp_fn(qt, hp, soff), ("sACT",), banks={3 * rot, 3 * rot + 1, 3 * rot + 2})

                if upto < 4:
                    continue
                S.w("vector", "sACT", M[("exp", qt, hp)]["sACT"])
                srot_ = (qt % 2) * 64
                h0_ = 2 * hp
                pre_ops = []
                if qt == QT0:
                    pre_ops.append((lambda sr, h0: (lambda eng: eng.tensor_tensor(
                        sums[:, sr + h0:sr + h0 + 2], sums[:, sr + h0:sr + h0 + 2],
                        sums[:, sr + 16 + h0:sr + 16 + h0 + 2], ALU.add)))(srot_, h0_))
                pre_ops.append((lambda sr, h0: (lambda eng: eng.reciprocal(
                    sums[:, sr + 32 + h0:sr + 32 + h0 + 2],
                    sums[:, sr + h0:sr + h0 + 2])))(srot_, h0_))

                def nm_fn(qtt, hpp):
                    srot = (qtt % 2) * 64
                    h0 = 2 * hpp
                    def f(eng):
                        r = None
                        for hh in range(2):
                            poff = (hpp % 2) * 1280 + hh * 640
                            inv = sums[:, srot + 32 + h0 + hh:srot + 33 + h0 + hh]
                            if qtt == QT0:
                                eng.tensor_scalar(Pb[:, poff:poff + 384], Pb[:, poff:poff + 384],
                                                  inv, None, ALU.mult)
                                r = eng.tensor_scalar(Pb[:, poff + 512:poff + 640], Pb[:, poff + 512:poff + 640],
                                                      inv, None, ALU.mult)
                            else:
                                r = eng.tensor_scalar(Pb[:, poff:poff + 640], Pb[:, poff:poff + 640],
                                                      inv, None, ALU.mult)
                        return r
                    return f
                M[("nm", qt, hp)] = S.chain("vector", pre_ops + [nm_fn(qt, hp)])

                if upto < 5:
                    continue
                S.w("tensor", "sDVE", M[("nm", qt, hp)]["sDVE"])
                pdst = rot * 3072

                def pt_fn(qtt, hpp):
                    prot = (hpp % 2) * 1280
                    def f(eng):
                        r = None
                        t = 0
                        for hh in range(2):
                            poff = prot + hh * 640
                            cols = [ki * 128 for ki in range(nkt)] + [512]
                            for cc in cols:
                                r = eng.transpose(PSb[:, pdst + t * 128:pdst + (t + 1) * 128],
                                                  Pb[:, poff + cc:poff + cc + 128], ident[:])
                                t += 1
                        return r
                    return f
                M[("ptp", qt, hp)] = S.e("tensor", pt_fn(qt, hp), ("sPE",), banks={3 * rot, 3 * rot + 1, 3 * rot + 2})

                S.w("vector", "sPE", M[("ptp", qt, hp)]["sPE"])
                ntile = 2 * (nkt + 1)

                def ptev_fn(qtt, hpp):
                    prot = (hpp % 2) * 1280
                    def f(eng):
                        return eng.tensor_copy(PTb[:, prot:prot + ntile * 128], PSb[:, pdst:pdst + ntile * 128])
                    return f
                M[("ptev", qt, hp)] = S.e("vector", ptev_fn(qt, hp), ("sDVE",), banks={3 * rot, 3 * rot + 1, 3 * rot + 2})

                if upto < 6:
                    continue
                S.w("tensor", "sDVE", M[("ptev", qt, hp)]["sDVE"])
                S.w("tensor", "sACT", M[("vev", qt)]["sACT"])
                if hp >= 2:
                    S.w("tensor", "sDVE", M[("otev", qt, hp - 2)]["sDVE"])
                elif qt > QT0:
                    S.w("tensor", "sDVE", M[("otev", qt - 1, 6 + hp)]["sDVE"])

                def av_fn(qtt, hpp):
                    prot = (hpp % 2) * 1280
                    ocol = 3072 + (hpp % 2) * 512
                    def f(eng):
                        r = None
                        nmm = nkt + 1
                        for hh in range(2):
                            h = 2 * hpp + hh
                            for ki in range(nmm):
                                ka = kts[ki] if ki < nkt else qtt
                                slot = ka % 8
                                lhs = v_ring[:, slot * 1024 + h * 64: slot * 1024 + h * 64 + 64]
                                rhs = PTb[:, prot + (hh * nmm + ki) * 128: prot + (hh * nmm + ki) * 128 + 128]
                                r = eng.matmul(PSf[0:64, ocol + hh * 128: ocol + hh * 128 + 128],
                                               lhs, rhs,
                                               start=(hh == 0 and ki == 0),
                                               stop=(hh == 1 and ki == nmm - 1))
                        return r
                    return f
                M[("av", qt, hp)] = S.e("tensor", av_fn(qt, hp), ("sPE",), banks={6 + rot})

                S.w("vector", "sPE", M[("av", qt, hp)]["sPE"])

                def otev_fn(qtt, hpp):
                    ocol = 3072 + (hpp % 2) * 512
                    def f(eng):
                        return eng.tensor_copy(oT[0:64, hpp * 256:hpp * 256 + 256],
                                               PSf[0:64, ocol:ocol + 256])
                    return f
                M[("otev", qt, hp)] = S.e("vector", otev_fn(qt, hp), ("sDVE",), banks={6 + rot})

            if upto < 7 or upto in (11, 13, 15):
                return
            S.w("tensor", "sDVE", M[("otev", qt, 7)]["sDVE"])
            S.w("tensor", "sACT", M[("exp", qt, 6)]["sACT"])

            def wo_fn(qtt):
                def f(eng):
                    r = None
                    for h in range(16):
                        lhs = oT[0:64, h * 128:(h + 1) * 128]
                        for half in range(2):
                            rhs = wout_b[0:64, h * 1024 + half * 512:h * 1024 + half * 512 + 512]
                            r = eng.matmul(PSf[:, half * 512:half * 512 + 512], lhs, rhs,
                                           start=(h == 0), stop=(h == 15))
                    return r
                return f
            M[("wo", qt)] = S.e("tensor", wo_fn(qt), ("sPE",), banks={0, 1})

            S.w("scalar", "sPE", M[("wo", qt)]["sPE"])
            M[("ssq2", qt)] = S.e("scalar",
                                  lambda eng: eng.activation(junk[:], PSf[:, 0:1024], AF.Square,
                                                             accum_out=st2[:, 1:2]),
                                  ("sACT",), banks={0, 1})
            S.w("vector", "sPE", M[("wo", qt)]["sPE"])
            S.w("vector", "sACT", M[("ssq2", qt)]["sACT"])
            pre2 = S.chain("vector", banks={0, 1}, fns=[
                lambda eng: eng.reduce_sum(st2[:, 0:1], PSf[:, 0:1024], axis=mybir.AxisListType.X),
                lambda eng: eng.tensor_scalar(st2[:, 2:3], st2[:, 0:1], 1.0 / 1024.0, None, ALU.mult),
                lambda eng: eng.tensor_tensor(st2[:, 3:4], st2[:, 2:3], st2[:, 2:3], ALU.mult),
            ])
            S.w("vector", "sDVE", pre2["sDVE"])
            M[("var2", qt)] = S.e("vector",
                                  lambda eng: eng.scalar_tensor_tensor(st2[:, 4:5], st2[:, 1:2], 1.0 / 1024.0,
                                                                       st2[:, 3:4], ALU.mult, ALU.subtract),
                                  ("sDVE",))
            S.w("scalar", "sDVE", M[("var2", qt)]["sDVE"])
            M[("rs2", qt)] = S.chain("scalar", [
                lambda eng: eng.activation(st2[:, 5:6], st2[:, 4:5], AF.Ln, bias=CE, scale=1.0),
                lambda eng: eng.activation(st2[:, 6:7], st2[:, 5:6], AF.Exp, bias=0.0, scale=CH),
            ])
            S.w("vector", "sACT", M[("rs2", qt)]["sACT"])
            M[("nb2", qt)] = S.e("vector",
                                 lambda eng: eng.scalar_tensor_tensor(st2[:, 7:8], st2[:, 2:3], -1.0,
                                                                      st2[:, 6:7], ALU.mult, ALU.mult),
                                 ("sDVE",))
            S.w("scalar", "sDVE", M[("nb2", qt)]["sDVE"])
            if qt > QT0:
                S.w("scalar", "sPOOL", M[("x2", qt - 1)]["sPOOL"])
            M[("t1", qt)] = S.e("scalar",
                                lambda eng: eng.activation(t1b[:], PSf[:, 0:1024], AF.Identity,
                                                           bias=st2[:, 7:8], scale=st2[:, 6:7]),
                                ("sACT",), banks={0, 1})

            S.w("gpsimd", "sACT", M[("t1", qt)]["sACT"])
            if qt >= QT0 + 2:
                psem = "sSPA" if (qt - 2) % 2 == 0 else "sSPB"
                S.w("gpsimd", psem, M[("spill", qt - 2)][psem])

            def x2_fn(qtt):
                def f(eng):
                    return eng.tensor_tensor(x2s[:, (qtt % 2) * 1024:(qtt % 2) * 1024 + 1024],
                                             x_tile(qtt), t1b[:], ALU.add)
                return f
            M[("x2", qt)] = S.e("gpsimd", x2_fn(qt), ("sPOOL",))
            S.w("sync", "sPOOL", M[("x2", qt)]["sPOOL"])
            spsem = "sSPA" if qt % 2 == 0 else "sSPB"
            M[("spill", qt)] = S.e("sync",
                                   (lambda qtt: (lambda eng: eng.dma_start(
                                       out=d_spill[(qtt - 3) * 128:(qtt - 2) * 128, :],
                                       in_=x2s[:, (qtt % 2) * 1024:(qtt % 2) * 1024 + 1024])))(qt),
                                   (spsem,), dma=True)

        emit_phase1()

        p1 = {s: S.c(s) for s in ("sPE", "sACT", "sDVE", "sPOOL")}

        def run_block():
            with nc.Block() as block:
                def runner(eng_obj, lst):
                    for item in lst:
                        if item[0] == "w":
                            eng_obj.wait_ge(sems[item[1]], item[2])
                        else:
                            inst = item[1](eng_obj)
                            for sem, amt in item[2]:
                                inst.then_inc(sems[sem], amt)

                @block.sync
                def _(e):
                    runner(e, S.ops["sync"])

                @block.tensor
                def _(e):
                    runner(e, S.ops["tensor"])

                @block.vector
                def _(e):
                    runner(e, S.ops["vector"])

                @block.scalar
                def _(e):
                    runner(e, S.ops["scalar"])

                @block.gpsimd
                def _(e):
                    runner(e, S.ops["gpsimd"])
            for k in ENGS:
                S.ops[k] = []

        run_block()
        es1.close()

        es2 = ExitStack()

        def sb2(name, shape, dt):
            return es2.enter_context(nc.sbuf_tensor(name, shape, dt))

        wr_b = sb2("wr_b", [128, 8 * 1024], BF16)
        wk_b = sb2("wk_b", [128, 8 * 1024], BF16)
        wv_b = sb2("wv_b", [128, 8 * 1024], BF16)
        stag2 = sb2("stag2", [128, 2 * 1024], F32)
        x2t = sb2("x2t", [128, 5 * 1024], F32)
        xl2 = sb2("xl2", [128, 2 * 1024], BF16)
        x2T = sb2("x2T", [128, 8 * 640], BF16)
        dbuf = sb2("dbuf", [128, 8 * 512], BF16)
        xrT = sb2("xrT", [128, 8 * 512], BF16)
        xkT = sb2("xkT", [128, 8 * 512], BF16)
        rbuf = sb2("rbuf", [128, 2 * 1024], F32)
        kbuf = sb2("kbuf", [128, 2 * 1024], BF16)
        kact = sb2("kact", [128, 2 * 1024], BF16)
        kactT = sb2("kactT", [128, 8 * 512], BF16)
        sgt = sb2("sgt", [128, 2 * 1024], F32)
        ffn = sb2("ffn", [128, 2 * 1024], F32)
        t2b = sb2("t2b", [128, 2 * 1024], F32)
        outb = sb2("outb", [128, 2 * 1024], F32)
        junk2 = sb2("junk2", [128, 1024], BF16)
        st3 = sb2("st3", [128, 16], F32)
        st4 = sb2("st4", [128, 16], F32)

        PS2 = es2.enter_context(nc.psum_tensor("PS2", [128, 4096], F32))
        PF = PS2.ap()
        PB2 = PS2.ap().bitcast(BF16)

        for eng in ENGS:
            for s in ("sPE", "sACT", "sDVE", "sPOOL"):
                S.w(eng, s, p1[s])

        for i in range(24):
            w = (d_wr, d_wk, d_wv)[i // 8]
            dstb = (wr_b, wk_b, wv_b)[i // 8]
            c = i % 8
            src = w[c * 128:(c + 1) * 128, :]
            dst = dstb[:, c * 1024:(c + 1) * 1024]
            slot = stag2[:, (i % 2) * 1024:(i % 2) * 1024 + 1024]
            wsem = "sWA" if i % 2 == 0 else "sWB"
            if i >= 2:
                S.w("sync", "sDVE", M[("cast2", i - 2)]["sDVE"])
            M[("wld2", i)] = S.e("sync",
                                 (lambda ss, hh: (lambda eng: eng.dma_start(out=ss, in_=hh)))(slot, src),
                                 (wsem,), dma=True)
            S.w("vector", wsem, M[("wld2", i)][wsem])
            M[("cast2", i)] = S.e("vector",
                                  (lambda ss, dd: (lambda eng: eng.tensor_copy(dd, ss)))(slot, dst),
                                  ("sDVE",))

        def st3c(tt, k):
            base = (tt % 2) * 8
            return st3[:, base + k:base + k + 1]

        def st4c(ot, k):
            base = (ot % 2) * 8
            return st4[:, base + k:base + k + 1]

        def emit_phase2():
            for sc in range(4):
                for tt in range(5):
                    gt = 4 * sc + tt
                    if sc > 0:
                        S.w("sync", "sDVE", M[("c2", sc - 1, tt)]["sDVE"])
                        if tt >= 1:
                            S.w("sync", "sPOOL", M[("ob", sc - 1, tt - 1)]["sPOOL"])
                    spsem = "sSPA" if (gt + 3) % 2 == 0 else "sSPB"
                    S.w("sync", spsem, M[("spill", gt + 3)][spsem])
                    if ("x2ld_prev",) in M:
                        S.w("sync", "sX2", M[("x2ld_prev",)]["sX2"])
                    M[("x2ld", sc, tt)] = S.e("sync",
                                              (lambda g, t: (lambda eng: eng.dma_start(
                                                  out=x2t[:, t * 1024:(t + 1) * 1024],
                                                  in_=d_spill[g * 128:(g + 1) * 128, :])))(gt, tt),
                                              ("sX2",), dma=True)
                    M[("x2ld_prev",)] = M[("x2ld", sc, tt)]
                    S.w("scalar", "sX2", M[("x2ld", sc, tt)]["sX2"])
                    M[("ssq3", sc, tt)] = S.e("scalar",
                                              (lambda t: (lambda eng: eng.activation(junk2[:], x2t[:, t * 1024:(t + 1) * 1024],
                                                                                     AF.Square, accum_out=st3c(t, 1))))(tt),
                                              ("sACT",))
                    S.w("vector", "sX2", M[("x2ld", sc, tt)]["sX2"])
                    pre3 = S.chain("vector", [
                        (lambda t: (lambda eng: eng.reduce_sum(st3c(t, 0), x2t[:, t * 1024:(t + 1) * 1024],
                                                               axis=mybir.AxisListType.X)))(tt),
                        (lambda t: (lambda eng: eng.tensor_scalar(st3c(t, 2), st3c(t, 0), 1.0 / 1024.0, None, ALU.mult)))(tt),
                        (lambda t: (lambda eng: eng.tensor_tensor(st3c(t, 3), st3c(t, 2), st3c(t, 2), ALU.mult)))(tt),
                    ])
                    S.w("vector", "sACT", M[("ssq3", sc, tt)]["sACT"])
                    S.w("vector", "sDVE", pre3["sDVE"])
                    M[("var3", sc, tt)] = S.e("vector",
                                              (lambda t: (lambda eng: eng.scalar_tensor_tensor(
                                                  st3c(t, 4), st3c(t, 1), 1.0 / 1024.0, st3c(t, 3),
                                                  ALU.mult, ALU.subtract)))(tt),
                                              ("sDVE",))
                    S.w("scalar", "sDVE", M[("var3", sc, tt)]["sDVE"])
                    M[("rs3", sc, tt)] = S.chain("scalar", [
                        (lambda t: (lambda eng: eng.activation(st3c(t, 5), st3c(t, 4), AF.Ln, bias=CE, scale=1.0)))(tt),
                        (lambda t: (lambda eng: eng.activation(st3c(t, 6), st3c(t, 5), AF.Exp, bias=0.0, scale=CH)))(tt),
                    ])
                    S.w("vector", "sACT", M[("rs3", sc, tt)]["sACT"])
                    M[("c2", sc, tt)] = S.chain("vector", [
                        (lambda t: (lambda eng: eng.scalar_tensor_tensor(st3c(t, 7), st3c(t, 2), -1.0, st3c(t, 6),
                                                                         ALU.mult, ALU.mult)))(tt),
                        (lambda t: (lambda eng: eng.tensor_scalar(xl2[:, (t % 2) * 1024:(t % 2) * 1024 + 1024],
                                                                  x2t[:, t * 1024:(t + 1) * 1024],
                                                                  st3c(t, 6), st3c(t, 7), ALU.mult, ALU.add)))(tt),
                    ])

                    S.w("tensor", "sDVE", M[("c2", sc, tt)]["sDVE"])
                    if (sc, tt) != (0, 0):
                        prev = (sc, tt - 1) if tt > 0 else (sc - 1, 4)
                        S.w("tensor", "sDVE", M[("e2", prev[0], prev[1])]["sDVE"])
                    if sc > 0 and tt == 0:
                        S.w("tensor", "sDVE", M[("ke", sc - 1, 3)]["sDVE"])

                    def t2_fn(t):
                        def f(eng):
                            r = None
                            for c in range(8):
                                r = eng.transpose(PB2[:, 6144 + c * 128:6272 + c * 128],
                                                  xl2[:, (t % 2) * 1024 + c * 128:(t % 2) * 1024 + (c + 1) * 128],
                                                  ident[:])
                            return r
                        return f
                    M[("pt2", sc, tt)] = S.e("tensor", t2_fn(tt), ("sPE",), banks={6})
                    S.w("vector", "sPE", M[("pt2", sc, tt)]["sPE"])

                    def e2_fn(t):
                        dst = bass.AP(x2T, t * 128, [[5120, 128], [640, 8], [1, 128]])

                        def f(eng):
                            return eng.tensor_copy(dst, PB2[:, 6144:7168].rearrange("p (c f) -> p c f", c=8))
                        return f
                    M[("e2", sc, tt)] = S.e("vector", e2_fn(tt), ("sDVE",), banks={6})

                if sc > 0:
                    S.w("vector", "sPE", M[("kmm", sc - 1, 3)]["sPE"])
                S.w("vector", "sDVE", M[("e2", sc, 4)]["sDVE"])

                def d_fn(eng):
                    src1 = bass.AP(x2T, 128, [[5120, 128], [640, 8], [1, 512]])
                    src0 = bass.AP(x2T, 127, [[5120, 128], [640, 8], [1, 512]])
                    dv = bass.AP(dbuf, 0, [[4096, 128], [512, 8], [1, 512]])
                    return eng.tensor_tensor(dv, src1, src0, ALU.subtract)

                def mix_fn(eng):
                    r = None
                    for c in range(8):
                        xp = bass.AP(x2T, c * 640 + 127, [[5120, 128], [1, 512]])
                        eng.scalar_tensor_tensor(xrT[:, c * 512:(c + 1) * 512],
                                                 dbuf[:, c * 512:(c + 1) * 512],
                                                 mus[:, c:c + 1], xp, ALU.mult, ALU.add)
                        r = eng.scalar_tensor_tensor(xkT[:, c * 512:(c + 1) * 512],
                                                     dbuf[:, c * 512:(c + 1) * 512],
                                                     mus[:, 8 + c:9 + c], xp, ALU.mult, ALU.add)
                    return r
                M[("mix", sc)] = S.chain("vector", [d_fn, mix_fn])

                for ot in range(4):
                    gt = 4 * sc + ot + 1
                    S.w("tensor", "sDVE", M[("mix", sc)]["sDVE"])
                    if sc + ot > 0:
                        prev = (sc, ot - 1) if ot > 0 else (sc - 1, 3)
                        S.w("tensor", "sACT", M[("relu", prev[0], prev[1])]["sACT"])
                    if ot == 0 and sc == 0:
                        S.w("tensor", "sDVE", M[("cast2", 23)]["sDVE"])

                    def rmm_fn(o):
                        def f(eng):
                            r = None
                            for c in range(8):
                                lhs = xrT[:, c * 512 + o * 128: c * 512 + o * 128 + 128]
                                for half in range(2):
                                    r = eng.matmul(PF[:, half * 512:half * 512 + 512], lhs,
                                                   wr_b[:, c * 1024 + half * 512: c * 1024 + half * 512 + 512],
                                                   start=(c == 0), stop=(c == 7))
                            return r
                        return f
                    M[("rmm", sc, ot)] = S.e("tensor", rmm_fn(ot), ("sPE",), banks={0, 1})

                    def kmm_fn(o):
                        def f(eng):
                            r = None
                            for c in range(8):
                                lhs = xkT[:, c * 512 + o * 128: c * 512 + o * 128 + 128]
                                for half in range(2):
                                    r = eng.matmul(PF[:, 1024 + half * 512:1024 + half * 512 + 512], lhs,
                                                   wk_b[:, c * 1024 + half * 512: c * 1024 + half * 512 + 512],
                                                   start=(c == 0), stop=(c == 7))
                            return r
                        return f
                    M[("kmm", sc, ot)] = S.e("tensor", kmm_fn(ot), ("sPE",), banks={2, 3})

                    S.w("scalar", "sPE", M[("rmm", sc, ot)]["sPE"])

                    def sigE_fn(o):
                        def f(eng):
                            return eng.activation(sgt[:, (o % 2) * 1024:(o % 2) * 1024 + 1024],
                                                  PF[:, 0:1024], AF.Exp, bias=0.0, scale=CN)
                        return f
                    M[("sigE", sc, ot)] = S.e("scalar", sigE_fn(ot), ("sACT",), banks={0, 1})
                    S.w("scalar", "sPE", M[("kmm", sc, ot)]["sPE"])

                    def relu_fn(o):
                        def f(eng):
                            return eng.activation(kbuf[:, (o % 2) * 1024:(o % 2) * 1024 + 1024],
                                                  PF[:, 1024:2048], AF.Relu)
                        return f
                    M[("relu", sc, ot)] = S.e("scalar", relu_fn(ot), ("sACT",), banks={2, 3})

                    S.w("vector", "sACT", M[("sigE", sc, ot)]["sACT"])

                    M[("sig", sc, ot)] = S.chain("vector", [
                        (lambda o: (lambda eng: eng.tensor_scalar(sgt[:, (o % 2) * 1024:(o % 2) * 1024 + 1024],
                                                                  sgt[:, (o % 2) * 1024:(o % 2) * 1024 + 1024],
                                                                  1.0, None, ALU.add)))(ot),
                        (lambda o: (lambda eng: eng.reciprocal(rbuf[:, (o % 2) * 1024:(o % 2) * 1024 + 1024],
                                                               sgt[:, (o % 2) * 1024:(o % 2) * 1024 + 1024])))(ot),
                    ])
                    S.w("vector", "sACT", M[("relu", sc, ot)]["sACT"])

                    def ka_fn(o):
                        def f(eng):
                            so = (o % 2) * 1024
                            return eng.tensor_tensor(kact[:, so:so + 1024], kbuf[:, so:so + 1024],
                                                     kbuf[:, so:so + 1024], ALU.mult)
                        return f
                    M[("ka", sc, ot)] = S.e("vector", ka_fn(ot), ("sDVE",))

                    S.w("tensor", "sDVE", M[("ka", sc, ot)]["sDVE"])
                    if sc + ot > 0:
                        prev = (sc, ot - 1) if ot > 0 else (sc - 1, 3)
                        S.w("tensor", "sDVE", M[("ke", prev[0], prev[1])]["sDVE"])

                    def tk2_fn(o):
                        def f(eng):
                            r = None
                            so = (o % 2) * 1024
                            for c in range(8):
                                r = eng.transpose(PB2[:, 7168 + c * 128:7296 + c * 128],
                                                  kact[:, so + c * 128: so + (c + 1) * 128], ident[:])
                            return r
                        return f
                    M[("ptk2", sc, ot)] = S.e("tensor", tk2_fn(ot), ("sPE",), banks={7})
                    S.w("vector", "sPE", M[("ptk2", sc, ot)]["sPE"])

                    def ke_fn(o):
                        dst = bass.AP(kactT, o * 128, [[4096, 128], [512, 8], [1, 128]])

                        def f(eng):
                            return eng.tensor_copy(dst, PB2[:, 7168:8192].rearrange("p (c f) -> p c f", c=8))
                        return f
                    M[("ke", sc, ot)] = S.e("vector", ke_fn(ot), ("sDVE",), banks={7})

                    S.w("tensor", "sDVE", M[("ke", sc, ot)]["sDVE"])
                    if sc + ot > 0:
                        prev = (sc, ot - 1) if ot > 0 else (sc - 1, 3)
                        S.w("tensor", "sDVE", M[("ff", prev[0], prev[1])]["sDVE"])

                    def vmm_fn(o):
                        def f(eng):
                            r = None
                            for c in range(8):
                                lhs = kactT[:, c * 512 + o * 128: c * 512 + o * 128 + 128]
                                for half in range(2):
                                    r = eng.matmul(PF[:, 2048 + half * 512:2048 + half * 512 + 512], lhs,
                                                   wv_b[:, c * 1024 + half * 512: c * 1024 + half * 512 + 512],
                                                   start=(c == 0), stop=(c == 7))
                            return r
                        return f
                    M[("vmm", sc, ot)] = S.e("tensor", vmm_fn(ot), ("sPE",), banks={4, 5})

                    S.w("vector", "sPE", M[("vmm", sc, ot)]["sPE"])
                    if sc * 4 + ot >= 2:
                        pr = (sc, ot - 2) if ot >= 2 else (sc - 1, 2 + ot)
                        S.w("vector", "sACT", M[("t2", pr[0], pr[1])]["sACT"])

                    S.w("vector", "sDVE", M[("sig", sc, ot)]["sDVE"])

                    def ff_fn(o):
                        def f(eng):
                            so = (o % 2) * 1024
                            return eng.tensor_tensor(ffn[:, so:so + 1024], PF[:, 2048:3072],
                                                     rbuf[:, so:so + 1024], ALU.mult)
                        return f
                    M[("ff", sc, ot)] = S.e("vector", ff_fn(ot), ("sDVE",), banks={4, 5})

                    S.w("scalar", "sDVE", M[("ff", sc, ot)]["sDVE"])

                    def ssq4_fn(o):
                        def f(eng):
                            so = (o % 2) * 1024
                            return eng.activation(junk2[:], ffn[:, so:so + 1024], AF.Square,
                                                  accum_out=st4c(o, 1))
                        return f
                    M[("ssq4", sc, ot)] = S.e("scalar", ssq4_fn(ot), ("sACT",))
                    S.w("vector", "sDVE", M[("ff", sc, ot)]["sDVE"])
                    pre4 = S.chain("vector", [
                        (lambda o: (lambda eng: eng.reduce_sum(st4c(o, 0),
                                                               ffn[:, (o % 2) * 1024:(o % 2) * 1024 + 1024],
                                                               axis=mybir.AxisListType.X)))(ot),
                        (lambda o: (lambda eng: eng.tensor_scalar(st4c(o, 2), st4c(o, 0), 1.0 / 1024.0, None, ALU.mult)))(ot),
                        (lambda o: (lambda eng: eng.tensor_tensor(st4c(o, 3), st4c(o, 2), st4c(o, 2), ALU.mult)))(ot),
                    ])
                    S.w("vector", "sACT", M[("ssq4", sc, ot)]["sACT"])
                    S.w("vector", "sDVE", pre4["sDVE"])
                    M[("var4", sc, ot)] = S.e("vector",
                                              (lambda o: (lambda eng: eng.scalar_tensor_tensor(
                                                  st4c(o, 4), st4c(o, 1), 1.0 / 1024.0, st4c(o, 3),
                                                  ALU.mult, ALU.subtract)))(ot),
                                              ("sDVE",))
                    S.w("scalar", "sDVE", M[("var4", sc, ot)]["sDVE"])
                    M[("rs4", sc, ot)] = S.chain("scalar", [
                        (lambda o: (lambda eng: eng.activation(st4c(o, 5), st4c(o, 4), AF.Ln, bias=CE, scale=1.0)))(ot),
                        (lambda o: (lambda eng: eng.activation(st4c(o, 6), st4c(o, 5), AF.Exp, bias=0.0, scale=CH)))(ot),
                    ])
                    S.w("vector", "sACT", M[("rs4", sc, ot)]["sACT"])
                    M[("nb4", sc, ot)] = S.e("vector",
                                             (lambda o: (lambda eng: eng.scalar_tensor_tensor(
                                                 st4c(o, 7), st4c(o, 2), -1.0, st4c(o, 6), ALU.mult, ALU.mult)))(ot),
                                             ("sDVE",))
                    S.w("scalar", "sDVE", M[("nb4", sc, ot)]["sDVE"])
                    if sc * 4 + ot >= 2:
                        pr = (sc, ot - 2) if ot >= 2 else (sc - 1, 2 + ot)
                        S.w("scalar", "sPOOL", M[("ob", pr[0], pr[1])]["sPOOL"])

                    def t2o_fn(o):
                        def f(eng):
                            so = (o % 2) * 1024
                            return eng.activation(t2b[:, so:so + 1024], ffn[:, so:so + 1024], AF.Identity,
                                                  bias=st4c(o, 7), scale=st4c(o, 6))
                        return f
                    M[("t2", sc, ot)] = S.e("scalar", t2o_fn(ot), ("sACT",))

                    S.w("gpsimd", "sACT", M[("t2", sc, ot)]["sACT"])
                    if sc * 4 + ot >= 2:
                        pr = (sc, ot - 2) if ot >= 2 else (sc - 1, 2 + ot)
                        stsem = "sSTA" if pr[1] % 2 == 0 else "sSTB"
                        S.w("gpsimd", stsem, M[("ost", pr[0], pr[1])][stsem])

                    def ob_fn(o):
                        def f(eng):
                            so = (o % 2) * 1024
                            return eng.tensor_tensor(outb[:, so:so + 1024],
                                                     x2t[:, (o + 1) * 1024:(o + 2) * 1024],
                                                     t2b[:, so:so + 1024], ALU.add)
                        return f
                    M[("ob", sc, ot)] = S.e("gpsimd", ob_fn(ot), ("sPOOL",))
                    S.w("sync", "sPOOL", M[("ob", sc, ot)]["sPOOL"])
                    grow = sc * 512 + ot * 128
                    stsem = "sSTA" if ot % 2 == 0 else "sSTB"
                    M[("ost", sc, ot)] = S.e("sync",
                                             (lambda g, o: (lambda eng: eng.dma_start(
                                                 out=d_out[g:g + 128, :],
                                                 in_=outb[:, (o % 2) * 1024:(o % 2) * 1024 + 1024])))(grow, ot),
                                             (stsem,), dma=True)

            S.w("sync", "sSTA", S.c("sSTA"))
            S.w("sync", "sSTB", S.c("sSTB"))

        if do_p2 and upto >= 99:
            emit_phase2()
        else:
            for sq in ("sSPA", "sSPB"):
                S.w("sync", sq, S.c(sq))
        run_block()
        es2.close()

        nc.all_engine_barrier()
        nc.gpsimd.dma_reset(sem_rng)
        nc.gpsimd.sem_clear(sem_rng)
        nc.all_engine_barrier()

    build_nc.last_sched = S
    return nc



def _prep_shared(inputs):
    w_qkv = np.ascontiguousarray(inputs["L0_attn_w_qkv"], dtype=np.float32)
    perm = np.zeros(D, dtype=np.int64)
    for h in range(NH):
        base = h * HD
        perm[base:base + 32] = base + np.arange(0, HD, 2)
        perm[base + 32:base + 64] = base + np.arange(1, HD, 2)
    wq = w_qkv[:, 0:D][:, perm]
    wk = w_qkv[:, D:2 * D][:, perm]
    wv = w_qkv[:, 2 * D:3 * D]
    wqkv_p = np.ascontiguousarray(np.concatenate([wq, wk, wv], axis=1))
    wo_raw = inputs["L0_attn_w_out"].astype(np.float32)
    wout = np.ascontiguousarray(wo_raw.reshape(16, 64, D).transpose(1, 0, 2).reshape(64, 16 * D))
    wr = np.ascontiguousarray(inputs["L0_ffn_W_r"].T, dtype=np.float32)
    wkf = np.ascontiguousarray(inputs["L0_ffn_W_k"].T, dtype=np.float32)
    wvf = np.ascontiguousarray(inputs["L0_ffn_W_v"].T, dtype=np.float32)
    mus = np.zeros((128, 16), np.float32)
    mus[:, 0:8] = inputs["L0_ffn_mu_r"].astype(np.float32).reshape(8, 128).T
    mus[:, 8:16] = inputs["L0_ffn_mu_k"].astype(np.float32).reshape(8, 128).T
    consts = np.zeros((128, 8), np.float32)
    consts[:, 0] = -BEXP
    consts[:, 1] = 0.125
    consts[:, 2] = EPS
    consts[:, 3] = -0.5
    consts[:, 4] = -1.0
    ident = np.eye(128, dtype=bfloat16)
    ii = np.arange(128)
    mfirst = np.where(ii[None, :] >= ii[:, None] + 1, 0.0, MASKV).astype(bfloat16)
    mlast = np.where(ii[None, :] <= ii[:, None], 0.0, MASKV).astype(bfloat16)
    return dict(wqkv=wqkv_p, wout=wout, wr=wr, wk=wkf, wv=wvf, mus=mus,
                consts=consts, ident=ident, mfirst=mfirst, mlast=mlast)


def _tabs_for(pos):
    p = np.maximum(pos, 0).astype(np.float64)
    inv_freq = 1.0 / 10000.0 ** (np.arange(0, HD, 2, dtype=np.float64) / HD)
    fr = p[:, None] * inv_freq[None, :]
    cos = np.cos(fr)
    sin = np.sin(fr)
    t = np.zeros((EXT, 128), np.float32)
    t[:, 0:32] = cos
    t[:, 32:64] = cos
    t[:, 64:96] = -sin
    t[:, 96:128] = sin
    return t


def _tabs_sbufmajor(t):
    out = np.zeros((128, EXT), np.float32)
    for j in range(NT):
        out[:, j * 128:(j + 1) * 128] = t[j * 128:(j + 1) * 128, :]
    return out


def _mq_for(half):
    ii = np.arange(128)
    mfirst = np.where(ii[None, :] >= ii[:, None] + 1, 0.0, MASKV).astype(np.float32)
    out = np.zeros((128, 2048), np.float32)
    for q in range(4):
        qt = 4 + q
        blk = np.zeros((128, 512), np.float32)
        if half == 1:
            blk[:, 0:128] = mfirst
        else:
            for ki in range(4):
                ka = qt - 4 + ki
                if ka < 4:
                    blk[:, ki * 128:(ki + 1) * 128] = MASKV
        out[:, q * 512:(q + 1) * 512] = blk
    return out.astype(bfloat16)


def kernel(L0_attn_x, L0_attn_w_qkv, L0_attn_w_out, L0_ffn_W_r, L0_ffn_W_k,
           L0_ffn_W_v, L0_ffn_mu_r, L0_ffn_mu_k):
    inputs = dict(L0_attn_x=np.asarray(L0_attn_x), L0_attn_w_qkv=np.asarray(L0_attn_w_qkv),
                  L0_attn_w_out=np.asarray(L0_attn_w_out), L0_ffn_W_r=np.asarray(L0_ffn_W_r),
                  L0_ffn_W_k=np.asarray(L0_ffn_W_k), L0_ffn_W_v=np.asarray(L0_ffn_W_v),
                  L0_ffn_mu_r=np.asarray(L0_ffn_mu_r), L0_ffn_mu_k=np.asarray(L0_ffn_mu_k))
    sh = _prep_shared(inputs)
    x = inputs["L0_attn_x"].astype(np.float32)

    tabs_e = _tabs_sbufmajor(_tabs_for(np.arange(-512, 2048)))
    tabs_o = _tabs_sbufmajor(_tabs_for(np.arange(1536, 4096)))
    mq_e = _mq_for(0)
    mq_o = _mq_for(1)

    in_maps = []
    for core in range(8):
        b, half = core // 2, core % 2
        if half == 0:
            x_ext = np.concatenate([np.zeros((512, D), np.float32), x[b, 0:2048]], axis=0)
        else:
            x_ext = np.ascontiguousarray(x[b, 1536:4096])
        in_maps.append({
            "x_ext": x_ext,
            "wqkv": sh["wqkv"], "wout": sh["wout"],
            "wr": sh["wr"], "wk": sh["wk"], "wv": sh["wv"],
            "tabs": (tabs_o if half else tabs_e),
            "mus": sh["mus"], "consts": sh["consts"],
            "ident": sh["ident"],
            "mfirst": sh["mfirst"], "mlast": sh["mlast"],
            "mq": (mq_o if half else mq_e),
        })

    nc = build_nc()
    res = run_bass_kernel_spmd(nc, in_maps, core_ids=list(range(8)))
    out = np.zeros((4, 4096, D), np.float32)
    for core in range(8):
        b, half = core // 2, core % 2
        out[b, half * 2048:(half + 1) * 2048] = res.results[core]["out"]
    return out


# version 06
# speedup vs baseline: 1.0021x; 1.0021x over previous
import math
import numpy as np
import ml_dtypes

import concourse.bass as bass
import concourse.mybir as mybir
from concourse.bass_utils import run_bass_kernel_spmd
from contextlib import ExitStack

F32 = mybir.dt.float32
BF16 = mybir.dt.bfloat16
AF = mybir.ActivationFunctionType
ALU = mybir.AluOpType
bfloat16 = ml_dtypes.bfloat16

D = 1024
NH = 16
HD = 64
EXT = 2560
NT = 20
QT0 = 3
NQT = 17
OUT_T = 2048
BEXP = 16.0
MASKV = -30000.0
EPS = 1e-5

ENGS = ("sync", "tensor", "vector", "scalar", "gpsimd")


class Sched:

    def __init__(self):
        self.cnt = {}
        self.ops = {e: [] for e in ENGS}
        self.trace = {e: [] for e in ENGS}
        self.label = None

    def w(self, eng, sem, val):
        if val > 0:
            self.ops[eng].append(("w", sem, val))
            self.trace[eng].append(("w", sem, val, self.label, frozenset()))

    def e(self, eng, fn, incs=(), dma=False, banks=()):
        out = {}
        amt = 16 if dma else 1
        incl = []
        for sem in incs:
            self.cnt[sem] = self.cnt.get(sem, 0) + amt
            incl.append((sem, amt))
            out[sem] = self.cnt[sem]
        self.ops[eng].append(("e", fn, incl))
        self.trace[eng].append(("e", incl, self.label, frozenset(banks)))
        return out

    def c(self, sem):
        return self.cnt.get(sem, 0)

    def chain(self, eng, fns, incs=(), banks=()):
        sem = {"vector": "sDVE", "scalar": "sACT", "tensor": "sPE",
               "gpsimd": "sPOOL"}[eng]

        def wrapper(eng_obj, _fns=tuple(fns)):
            r = None
            for i, fn in enumerate(_fns):
                if i:
                    eng_obj.drain()
                r = fn(eng_obj)
            return r
        want = set(incs) | {sem}
        return self.e(eng, wrapper, tuple(sorted(want)), banks=banks)


def build_nc(nt=NT, do_attn=True, do_p2=True, upto=99):
    nc = bass.Bass("TRN2", target_bir_lowering=False)

    d_x = nc.dram_tensor("x_ext", [EXT, D], F32, kind="ExternalInput").ap()
    d_wqkv = nc.dram_tensor("wqkv", [D, 3 * D], F32, kind="ExternalInput").ap()
    d_wout = nc.dram_tensor("wout", [64, 16 * 1024], F32, kind="ExternalInput").ap()
    d_wr = nc.dram_tensor("wr", [D, D], F32, kind="ExternalInput").ap()
    d_wk = nc.dram_tensor("wk", [D, D], F32, kind="ExternalInput").ap()
    d_wv = nc.dram_tensor("wv", [D, D], F32, kind="ExternalInput").ap()
    d_tabs = nc.dram_tensor("tabs", [128, EXT], F32, kind="ExternalInput").ap()
    d_mus = nc.dram_tensor("mus", [128, 16], F32, kind="ExternalInput").ap()
    d_consts = nc.dram_tensor("consts", [128, 8], F32, kind="ExternalInput").ap()
    d_ident = nc.dram_tensor("ident", [128, 128], BF16, kind="ExternalInput").ap()
    d_mfirst = nc.dram_tensor("mfirst", [128, 128], BF16, kind="ExternalInput").ap()
    d_mlast = nc.dram_tensor("mlast", [128, 128], BF16, kind="ExternalInput").ap()
    d_mq = nc.dram_tensor("mq", [128, 2048], BF16, kind="ExternalInput").ap()
    d_out = nc.dram_tensor("out", [OUT_T, D], F32, kind="ExternalOutput").ap()
    d_spill = nc.dram_tensor("x2spill", [17 * 128, D], F32, kind="Internal").ap()

    es = ExitStack()

    def sb(name, shape, dt):
        return es.enter_context(nc.sbuf_tensor(name, shape, dt))

    consts = sb("sb_consts", [128, 8], F32)
    mus = sb("sb_mus", [128, 16], F32)
    ident = sb("sb_ident", [128, 128], BF16)

    sems = {}
    for s in ("sSET", "sWA", "sWB", "sXA", "sXB", "sSPA", "sSPB", "sX2",
              "sSTA", "sSTB", "sPE", "sACT", "sDVE", "sPOOL"):
        sems[s] = es.enter_context(nc.semaphore(s))

    sem_nums = sorted(h.num for h in sems.values())
    sem_rng = range(sem_nums[0], sem_nums[-1] + 1)
    assert len(sem_nums) == len(sem_rng), "semaphore numbers not contiguous"
    nc.all_engine_barrier()
    nc.gpsimd.dma_reset(sem_rng)
    nc.gpsimd.sem_clear(sem_rng)
    nc.all_engine_barrier()

    S = Sched()
    M = {}

    CB = consts[:, 0:1]
    CS = consts[:, 1:2]
    CE = consts[:, 2:3]
    CH = consts[:, 3:4]
    CN = consts[:, 4:5]

    with es:
        es1 = ExitStack()

        def sb1(name, shape, dt):
            return es1.enter_context(nc.sbuf_tensor(name, shape, dt))

        wqkv_b = sb1("wqkv_b", [128, 8 * 3072], BF16)
        wout_b = sb1("wout_b", [64, 16 * 1024], BF16)
        stag = sb1("stag", [128, 2 * 1024], F32)
        kT_ring = sb1("kT_ring", [128, 8 * 1024], BF16)
        v_ring = sb1("v_ring", [128, 8 * 1024], BF16)
        qTp = sb1("qTp", [128, 2 * 2048], BF16)
        xb = sb1("xb", [128, 2 * 1024], F32)
        xln1 = sb1("xln1", [128, 1024], BF16)
        xlt = sb1("xlt", [128, 1024], BF16)
        tabs = sb1("sb_tabs", [128, 2560], F32)
        mfirst = sb1("sb_mfirst", [128, 128], BF16)
        mlast = sb1("sb_mlast", [128, 128], BF16)
        mq = sb1("sb_mq", [128, 2048], BF16)
        tmpA = sb1("tmpA", [128, 2048], F32)
        tmpB = sb1("tmpB", [128, 2048], F32)
        qnat = sb1("qnat", [128, 1024], BF16)
        knat = sb1("knat", [128, 1024], BF16)
        Pb = sb1("Pb", [128, 2 * 1280], BF16)
        PTb = sb1("PTb", [128, 2 * 1280], BF16)
        oT = sb1("oT", [128, 2048], BF16)
        sums = sb1("sums", [128, 128], F32)
        t1b = sb1("t1b", [128, 1024], F32)
        x2s = sb1("x2s", [128, 2 * 1024], F32)
        junk = sb1("junk", [128, 1024], BF16)
        st1 = sb1("st1", [128, 32], F32)
        st2 = sb1("st2", [128, 16], F32)

        PS = es1.enter_context(nc.psum_tensor("PS", [128, 4096], F32))
        PSf = PS.ap()
        PSb = PS.ap().bitcast(BF16)

        setup_ld = None
        for t, h in ((consts, d_consts), (mus, d_mus), (ident, d_ident),
                     (tabs, d_tabs), (mfirst, d_mfirst), (mlast, d_mlast), (mq, d_mq)):
            setup_ld = S.e("sync", (lambda tt, hh: (lambda eng: eng.dma_start(out=tt[:], in_=hh)))(t, h),
                           ("sSET",), dma=True)
        for eng in ("tensor", "vector", "scalar", "gpsimd"):
            S.w(eng, "sSET", setup_ld["sSET"])
        S.e("vector", lambda eng: eng.memset(qTp[:], 0.0), ("sDVE",))

        witems = []
        for c in range(8):
            for hh in range(3):
                witems.append((d_wqkv[c * 128:(c + 1) * 128, hh * 1024:(hh + 1) * 1024],
                               wqkv_b[:, c * 3072 + hh * 1024:c * 3072 + (hh + 1) * 1024], 128))
        for c in range(16):
            witems.append((d_wout[:, c * 1024:(c + 1) * 1024],
                           wout_b[0:64, c * 1024:(c + 1) * 1024], 64))
        for i, (src, dst, nparts) in enumerate(witems):
            slot = stag[0:nparts, (i % 2) * 1024:(i % 2) * 1024 + 1024]
            wsem = "sWA" if i % 2 == 0 else "sWB"
            if i >= 2:
                S.w("sync", "sDVE", M[("cast", i - 2)]["sDVE"])
            M[("wld", i)] = S.e("sync",
                                (lambda ss, hh: (lambda eng: eng.dma_start(out=ss, in_=hh)))(slot, src),
                                (wsem,), dma=True)
            S.w("vector", wsem, M[("wld", i)][wsem])
            M[("cast", i)] = S.e("vector",
                                 (lambda ss, dd: (lambda eng: eng.tensor_copy(dd, ss)))(slot, dst),
                                 ("sDVE",))

        def x_tile(j):
            return xb[:, (j % 2) * 1024:(j % 2) * 1024 + 1024]

        def st1c(j, k):
            base = (j % 2) * 8
            return st1[:, base + k:base + k + 1]

        def emit_phase1():
            for j in range(nt):
                qt = j if j >= QT0 else None
                rot = j % 2

                if j >= 2:
                    if ("x2", j - 2) in M:
                        S.w("sync", "sPOOL", M[("x2", j - 2)]["sPOOL"])
                    else:
                        S.w("sync", "sDVE", M[("c1", j - 2)]["sDVE"])
                        S.w("sync", "sACT", M[("ssq", j - 2)]["sACT"])
                xsem = "sXA" if j % 2 == 0 else "sXB"
                M[("xld", j)] = S.e("sync",
                                    (lambda jj: (lambda eng: eng.dma_start(out=x_tile(jj), in_=d_x[jj * 128:(jj + 1) * 128, :])))(j),
                                    (xsem,), dma=True)

                S.w("scalar", xsem, M[("xld", j)][xsem])
                M[("ssq", j)] = S.e("scalar",
                                    (lambda jj: (lambda eng: eng.activation(junk[:], x_tile(jj), AF.Square,
                                                                            accum_out=st1c(jj, 1))))(j),
                                    ("sACT",))
                S.w("vector", xsem, M[("xld", j)][xsem])
                pre = S.chain("vector", [
                    (lambda jj: (lambda eng: eng.reduce_sum(st1c(jj, 0), x_tile(jj), axis=mybir.AxisListType.X)))(j),
                    (lambda jj: (lambda eng: eng.tensor_scalar(st1c(jj, 2), st1c(jj, 0), 1.0 / 1024.0, None, ALU.mult)))(j),
                    (lambda jj: (lambda eng: eng.tensor_tensor(st1c(jj, 3), st1c(jj, 2), st1c(jj, 2), ALU.mult)))(j),
                ])
                S.w("vector", "sACT", M[("ssq", j)]["sACT"])
                S.w("vector", "sDVE", pre["sDVE"])
                M[("var", j)] = S.e("vector",
                                    (lambda jj: (lambda eng: eng.scalar_tensor_tensor(st1c(jj, 4), st1c(jj, 1), 1.0 / 1024.0,
                                                                                      st1c(jj, 3), ALU.mult, ALU.subtract)))(j),
                                    ("sDVE",))
                S.w("scalar", "sDVE", M[("var", j)]["sDVE"])
                M[("rs1", j)] = S.chain("scalar", [
                    (lambda jj: (lambda eng: eng.activation(st1c(jj, 5), st1c(jj, 4), AF.Ln, bias=CE, scale=1.0)))(j),
                    (lambda jj: (lambda eng: eng.activation(st1c(jj, 6), st1c(jj, 5), AF.Exp, bias=0.0, scale=CH)))(j),
                ])
                S.w("vector", "sACT", M[("rs1", j)]["sACT"])
                M[("c1", j)] = S.chain("vector", [
                    (lambda jj: (lambda eng: eng.scalar_tensor_tensor(st1c(jj, 7), st1c(jj, 2), -1.0, st1c(jj, 6),
                                                                      ALU.mult, ALU.mult)))(j),
                    (lambda jj: (lambda eng: eng.tensor_scalar(xln1[:], x_tile(jj), st1c(jj, 6), st1c(jj, 7),
                                                               ALU.mult, ALU.add)))(j),
                ])

                S.w("tensor", "sDVE", M[("c1", j)]["sDVE"])
                if j > 0:
                    S.w("tensor", "sDVE", M[("ktev", j - 1)]["sDVE"])
                if ("otev", j - 1, 7) in M:
                    S.w("tensor", "sDVE", M[("otev", j - 1, 7)]["sDVE"])

                def t1_fn(jj):
                    def f(eng):
                        r = None
                        for c in range(8):
                            r = eng.transpose(PSb[:, 6144 + c * 128:6272 + c * 128],
                                              xln1[:, c * 128:(c + 1) * 128], ident[:])
                        return r
                    return f
                M[("pt1", j)] = S.e("tensor", t1_fn(j), ("sPE",), banks={6})

                S.w("vector", "sPE", M[("pt1", j)]["sPE"])
                M[("e1", j)] = S.e("vector", lambda eng: eng.tensor_copy(xlt[:], PSb[:, 6144:7168]), ("sDVE",), banks={6})

                S.w("tensor", "sDVE", M[("e1", j)]["sDVE"])
                if j == 0:
                    S.w("tensor", "sDVE", M[("cast", 39)]["sDVE"])
                if ("t1", j - 1) in M:
                    S.w("tensor", "sACT", M[("t1", j - 1)]["sACT"])
                    S.w("tensor", "sDVE", M[("nb2", j - 1)]["sDVE"])
                elif j > 0:
                    S.w("tensor", "sACT", M[("vev", j - 1)]["sACT"])
                    if ("ropeK", j - 1, "d") in M:
                        S.w("tensor", "sDVE", M[("ropeK", j - 1, "d")]["sDVE"])

                def qkv_fn(jj):
                    def f(eng):
                        r = None
                        for c in range(8):
                            lhs = xlt[:, c * 128:(c + 1) * 128]
                            for half in range(2):
                                for w in range(3):
                                    r = eng.matmul(PSf[:, w * 1024 + half * 512: w * 1024 + half * 512 + 512],
                                                   lhs,
                                                   wqkv_b[:, c * 3072 + w * 1024 + half * 512: c * 3072 + w * 1024 + half * 512 + 512],
                                                   start=(c == 0), stop=(c == 7))
                        return r
                    return f
                M[("qkv", j)] = S.e("tensor", qkv_fn(j), ("sPE",), banks={0, 1, 2, 3, 4, 5})

                S.w("scalar", "sPE", M[("qkv", j)]["sPE"])
                M[("vev", j)] = S.e("scalar",
                                    (lambda jj: (lambda eng: eng.activation(v_ring[:, (jj % 8) * 1024:(jj % 8) * 1024 + 1024],
                                                                            PSf[:, 2048:3072], AF.Copy)))(j),
                                    ("sACT",), banks={4, 5})

                def rope_fn(jj, base_off, toff):
                    cos_ap = bass.AP(tabs, jj * 128, [[2560, 128], [0, 16], [0, 2], [1, 32]])
                    sin_ap = bass.AP(tabs, jj * 128 + 64, [[2560, 128], [0, 16], [32, 2], [1, 32]])
                    plain = bass.AP(PS, base_off, [[4096, 128], [64, 16], [32, 2], [1, 32]])
                    swap = bass.AP(PS, base_off + 32, [[4096, 128], [64, 16], [-32, 2], [1, 32]])
                    tA = bass.AP(tmpA, toff, [[2048, 128], [64, 16], [32, 2], [1, 32]])
                    tB = bass.AP(tmpB, toff, [[2048, 128], [64, 16], [32, 2], [1, 32]])

                    def f(eng):
                        eng.tensor_tensor(tA, plain, cos_ap, ALU.mult)
                        return eng.tensor_tensor(tB, swap, sin_ap, ALU.mult)
                    return f

                S.w("vector", "sPE", M[("qkv", j)]["sPE"])
                if upto == 21:
                    M[("ropeK", j, "g")] = M[("qkv", j)]
                    M[("ktev", j)] = M[("c1", j)]
                    M[("qtev", j)] = M[("c1", j)]
                if j >= QT0 and upto != 21:
                    M[("ropeQ", j, "g")] = S.chain("vector", [
                        rope_fn(j, 0, 0),
                        lambda eng: eng.tensor_tensor(qnat[:], tmpA[:, 0:1024], tmpB[:, 0:1024], ALU.add),
                    ], banks={0, 1})
                    M[("ropeQ", j, "d")] = M[("ropeQ", j, "g")]

                if upto != 21:
                    M[("ropeK", j, "g")] = S.chain("vector", [
                        rope_fn(j, 1024, 1024),
                        lambda eng: eng.tensor_tensor(knat[:], tmpA[:, 1024:2048], tmpB[:, 1024:2048], ALU.add),
                    ], banks={2, 3})
                    M[("ropeK", j, "d")] = M[("ropeK", j, "g")]

                if j >= QT0 and upto != 21:
                    S.w("tensor", "sDVE", M[("ropeQ", j, "g")]["sDVE"])
                    if j > QT0:
                        S.w("tensor", "sDVE", M[("qtev", j - 1)]["sDVE"])

                    def tq_fn(jj):
                        def f(eng):
                            r = None
                            for c in range(8):
                                r = eng.transpose(PSb[:, 7168 + c * 128:7296 + c * 128],
                                                  qnat[:, c * 128:(c + 1) * 128], ident[:])
                            return r
                        return f
                    M[("ptq", j)] = S.e("tensor", tq_fn(j), ("sPE",), banks={7})
                    S.w("vector", "sPE", M[("ptq", j)]["sPE"])
                    def qtev_fn(jj):
                        rot = jj % 2
                        def f(eng):
                            srcA = bass.AP(PS, 3584, [[4096, 64], [64, 8], [1, 64]]).bitcast(BF16) if False else None
                            sA = bass.AP(PSb.tensor if hasattr(PSb, "tensor") else PS, 0, [])
                            a_src = bass.AP(PS, 3584, [[2048, 64], [64, 8], [1, 64]])
                            return eng.tensor_copy(
                                bass.AP(qTp, rot * 2048, [[4096, 64], [256, 8], [1, 128]]),
                                a_src.bitcast(BF16))
                        return f
                    def qtev2_fn(jj):
                        rot = jj % 2
                        def f(eng):
                            eng.tensor_copy(
                                bass.AP(qTp, rot * 2048, [[4096, 64], [256, 8], [1, 128]]),
                                bass.AP(qTp, 0, []) if False else
                                PSb[0:64, 7168:8192].rearrange("p (c f) -> p c f", c=8))
                            return eng.tensor_copy(
                                bass.AP(qTp, 4096 * 64 + rot * 2048 + 128, [[4096, 64], [256, 8], [1, 128]]),
                                PSb[64:128, 7168:8192].rearrange("p (c f) -> p c f", c=8))
                        return f
                    M[("qtev", j)] = S.e("vector", qtev2_fn(j), ("sDVE",), banks={7})

                if upto != 21:
                    S.w("tensor", "sDVE", M[("ropeK", j, "g")]["sDVE"])
                S.w("tensor", "sDVE", M[("e1", j)]["sDVE"])

                if upto != 21:
                    def tk_fn(jj):
                        def f(eng):
                            r = None
                            for c in range(8):
                                r = eng.transpose(PSb[:, 6144 + c * 128:6272 + c * 128],
                                                  knat[:, c * 128:(c + 1) * 128], ident[:])
                            return r
                        return f
                    M[("ptk", j)] = S.e("tensor", tk_fn(j), ("sPE",), banks={6})
                    S.w("vector", "sPE", M[("ptk", j)]["sPE"])
                    M[("ktev", j)] = S.e("vector",
                                         (lambda jj: (lambda eng: eng.tensor_copy(kT_ring[:, (jj % 8) * 1024:(jj % 8) * 1024 + 1024],
                                                                                  PSb[:, 6144:7168])))(j),
                                         ("sDVE",), banks={6})

                if qt is not None and do_attn:
                    emit_attn(qt, upto)

        def emit_attn(qt, upto=99):
            j = qt
            kts = list(range(max(0, qt - 4), qt))
            nkt = len(kts)
            qclass = qt - 4 if 4 <= qt <= 7 else None

            for hp in range(8):
                if False:
                    for ss in ("sDVE", "sACT", "sPOOL"):
                        S.w("tensor", ss, S.c(ss))
                    for ss in ("sPE", "sACT"):
                        S.w("vector", ss, S.c(ss))
                    for ss in ("sPE", "sDVE"):
                        S.w("scalar", ss, S.c(ss))
                rot = hp % 2
                soff = rot * 1536
                S.w("tensor", "sDVE", M[("qtev", qt)]["sDVE"])
                S.w("tensor", "sDVE", M[("ktev", qt)]["sDVE"])
                if hp == 1:
                    S.w("tensor", "sACT", M[("vev", qt)]["sACT"])
                for key in ((qt, hp - 2) if hp >= 2 else (qt - 1, 6 + hp),):
                    for stage, sem in (("exp", "sACT"), ("ptev", "sDVE"), ("nm", "sDVE"),
                                       ("mk", "sDVE")):
                        if (stage, key[0], key[1]) in M:
                            S.w("tensor", sem, M[(stage, key[0], key[1])][sem])
                            if stage in ("exp",):
                                continue
                            break
                    else:
                        continue

                def sc_fn(qtt, hpp, so):
                    def f(eng):
                        r = None
                        for hh in range(2):
                            h = 2 * hpp + hh
                            lhs = qTp[:, (qtt % 2) * 2048 + h * 128:(qtt % 2) * 2048 + (h + 1) * 128]
                            hoff = so + hh * 640
                            for ki, ka in enumerate(kts):
                                slot = ka % 8
                                rhs = kT_ring[:, slot * 1024 + hpp * 128: slot * 1024 + hpp * 128 + 128]
                                r = eng.matmul(PSf[:, hoff + ki * 128: hoff + ki * 128 + 128],
                                               lhs, rhs, start=True, stop=True)
                            slot = qtt % 8
                            rhs = kT_ring[:, slot * 1024 + hpp * 128: slot * 1024 + hpp * 128 + 128]
                            r = eng.matmul(PSf[:, hoff + 512: hoff + 640],
                                           lhs, rhs, start=True, stop=True)
                        return r
                    return f
                M[("sc", qt, hp)] = S.e("tensor", sc_fn(qt, hp, soff), ("sPE",), banks={3 * rot, 3 * rot + 1, 3 * rot + 2})

                if upto in (13, 15) and hp > 0:
                    break
                if upto < 2 or upto in (11, 13, 15):
                    continue
                S.w("vector", "sPE", M[("sc", qt, hp)]["sPE"])

                def mk_fn(qtt, hpp, so):
                    def f(eng):
                        if upto == 12:
                            return eng.tensor_copy(junk[:, 0:128], xln1[:, 0:128])
                        r = None
                        if qclass is not None:
                            src = bass.AP(PS, so, [[4096, 128], [640, 2], [1, 512]])
                            m = bass.AP(mq, qclass * 512, [[2048, 128], [0, 2], [1, 512]])
                            r = eng.tensor_tensor(src, src, m, ALU.add)
                        elif qtt >= 8:
                            src = bass.AP(PS, so, [[4096, 128], [640, 2], [1, 128]])
                            m = bass.AP(mfirst, 0, [[128, 128], [0, 2], [1, 128]])
                            r = eng.tensor_tensor(src, src, m, ALU.add)
                        srcd = bass.AP(PS, so + 512, [[4096, 128], [640, 2], [1, 128]])
                        md = bass.AP(mlast, 0, [[128, 128], [0, 2], [1, 128]])
                        return eng.tensor_tensor(srcd, srcd, md, ALU.add)
                    return f
                M[("mk", qt, hp)] = S.e("vector", mk_fn(qt, hp, soff), ("sDVE",), banks={3 * rot, 3 * rot + 1, 3 * rot + 2})

                if upto < 3:
                    continue
                S.w("scalar", "sDVE", M[("mk", qt, hp)]["sDVE"])

                def exp_fn(qtt, hpp, so):
                    srot = (qtt % 2) * 64
                    def f(eng):
                        r = None
                        for hh in range(2):
                            h = 2 * hpp + hh
                            hoff = so + hh * 640
                            poff = (hpp % 2) * 1280 + hh * 640
                            if qtt == QT0:
                                eng.activation(Pb[:, poff:poff + 384], PSf[:, hoff:hoff + 384],
                                               AF.Exp, bias=CB, scale=CS,
                                               accum_out=sums[:, srot + h:srot + h + 1])
                                r = eng.activation(Pb[:, poff + 512:poff + 640], PSf[:, hoff + 512:hoff + 640],
                                                   AF.Exp, bias=CB, scale=CS,
                                                   accum_out=sums[:, srot + 16 + h:srot + 16 + h + 1])
                            else:
                                r = eng.activation(Pb[:, poff:poff + 640], PSf[:, hoff:hoff + 640],
                                                   AF.Exp, bias=CB, scale=CS,
                                                   accum_out=sums[:, srot + h:srot + h + 1])
                        return r
                    return f
                M[("exp", qt, hp)] = S.e("scalar", exp_fn(qt, hp, soff), ("sACT",), banks={3 * rot, 3 * rot + 1, 3 * rot + 2})

                if upto < 4:
                    continue
                S.w("vector", "sACT", M[("exp", qt, hp)]["sACT"])
                srot_ = (qt % 2) * 64
                h0_ = 2 * hp
                pre_ops = []
                if qt == QT0:
                    pre_ops.append((lambda sr, h0: (lambda eng: eng.tensor_tensor(
                        sums[:, sr + h0:sr + h0 + 2], sums[:, sr + h0:sr + h0 + 2],
                        sums[:, sr + 16 + h0:sr + 16 + h0 + 2], ALU.add)))(srot_, h0_))
                pre_ops.append((lambda sr, h0: (lambda eng: eng.reciprocal(
                    sums[:, sr + 32 + h0:sr + 32 + h0 + 2],
                    sums[:, sr + h0:sr + h0 + 2])))(srot_, h0_))

                def nm_fn(qtt, hpp):
                    srot = (qtt % 2) * 64
                    h0 = 2 * hpp
                    def f(eng):
                        r = None
                        for hh in range(2):
                            poff = (hpp % 2) * 1280 + hh * 640
                            inv = sums[:, srot + 32 + h0 + hh:srot + 33 + h0 + hh]
                            if qtt == QT0:
                                eng.tensor_scalar(Pb[:, poff:poff + 384], Pb[:, poff:poff + 384],
                                                  inv, None, ALU.mult)
                                r = eng.tensor_scalar(Pb[:, poff + 512:poff + 640], Pb[:, poff + 512:poff + 640],
                                                      inv, None, ALU.mult)
                            else:
                                r = eng.tensor_scalar(Pb[:, poff:poff + 640], Pb[:, poff:poff + 640],
                                                      inv, None, ALU.mult)
                        return r
                    return f
                M[("nm", qt, hp)] = S.chain("vector", pre_ops + [nm_fn(qt, hp)])

                if upto < 5:
                    continue
                S.w("tensor", "sDVE", M[("nm", qt, hp)]["sDVE"])
                pdst = rot * 3072

                def pt_fn(qtt, hpp):
                    prot = (hpp % 2) * 1280
                    def f(eng):
                        r = None
                        t = 0
                        for hh in range(2):
                            poff = prot + hh * 640
                            cols = [ki * 128 for ki in range(nkt)] + [512]
                            for cc in cols:
                                r = eng.transpose(PSb[:, pdst + t * 128:pdst + (t + 1) * 128],
                                                  Pb[:, poff + cc:poff + cc + 128], ident[:])
                                t += 1
                        return r
                    return f
                M[("ptp", qt, hp)] = S.e("tensor", pt_fn(qt, hp), ("sPE",), banks={3 * rot, 3 * rot + 1, 3 * rot + 2})

                S.w("vector", "sPE", M[("ptp", qt, hp)]["sPE"])
                ntile = 2 * (nkt + 1)

                def ptev_fn(qtt, hpp):
                    prot = (hpp % 2) * 1280
                    def f(eng):
                        return eng.tensor_copy(PTb[:, prot:prot + ntile * 128], PSb[:, pdst:pdst + ntile * 128])
                    return f
                M[("ptev", qt, hp)] = S.e("vector", ptev_fn(qt, hp), ("sDVE",), banks={3 * rot, 3 * rot + 1, 3 * rot + 2})

                if upto < 6:
                    continue
                S.w("tensor", "sDVE", M[("ptev", qt, hp)]["sDVE"])
                S.w("tensor", "sACT", M[("vev", qt)]["sACT"])
                if hp >= 2:
                    S.w("tensor", "sDVE", M[("otev", qt, hp - 2)]["sDVE"])
                elif qt > QT0:
                    S.w("tensor", "sDVE", M[("otev", qt - 1, 6 + hp)]["sDVE"])

                def av_fn(qtt, hpp):
                    prot = (hpp % 2) * 1280
                    ocol = 3072 + (hpp % 2) * 512
                    def f(eng):
                        r = None
                        nmm = nkt + 1
                        for hh in range(2):
                            h = 2 * hpp + hh
                            for ki in range(nmm):
                                ka = kts[ki] if ki < nkt else qtt
                                slot = ka % 8
                                lhs = v_ring[:, slot * 1024 + h * 64: slot * 1024 + h * 64 + 64]
                                rhs = PTb[:, prot + (hh * nmm + ki) * 128: prot + (hh * nmm + ki) * 128 + 128]
                                r = eng.matmul(PSf[0:64, ocol + hh * 128: ocol + hh * 128 + 128],
                                               lhs, rhs,
                                               start=(hh == 0 and ki == 0),
                                               stop=(hh == 1 and ki == nmm - 1))
                        return r
                    return f
                M[("av", qt, hp)] = S.e("tensor", av_fn(qt, hp), ("sPE",), banks={6 + rot})

                S.w("vector", "sPE", M[("av", qt, hp)]["sPE"])

                def otev_fn(qtt, hpp):
                    ocol = 3072 + (hpp % 2) * 512
                    def f(eng):
                        return eng.tensor_copy(oT[0:64, hpp * 256:hpp * 256 + 256],
                                               PSf[0:64, ocol:ocol + 256])
                    return f
                M[("otev", qt, hp)] = S.e("vector", otev_fn(qt, hp), ("sDVE",), banks={6 + rot})

            if upto < 7 or upto in (11, 13, 15):
                return
            S.w("tensor", "sDVE", M[("otev", qt, 7)]["sDVE"])
            S.w("tensor", "sACT", M[("exp", qt, 6)]["sACT"])

            def wo_fn(qtt):
                def f(eng):
                    r = None
                    for h in range(16):
                        lhs = oT[0:64, h * 128:(h + 1) * 128]
                        for half in range(2):
                            rhs = wout_b[0:64, h * 1024 + half * 512:h * 1024 + half * 512 + 512]
                            r = eng.matmul(PSf[:, half * 512:half * 512 + 512], lhs, rhs,
                                           start=(h == 0), stop=(h == 15))
                    return r
                return f
            M[("wo", qt)] = S.e("tensor", wo_fn(qt), ("sPE",), banks={0, 1})

            S.w("scalar", "sPE", M[("wo", qt)]["sPE"])
            M[("ssq2", qt)] = S.e("scalar",
                                  lambda eng: eng.activation(junk[:], PSf[:, 0:1024], AF.Square,
                                                             accum_out=st2[:, 1:2]),
                                  ("sACT",), banks={0, 1})
            S.w("vector", "sPE", M[("wo", qt)]["sPE"])
            S.w("vector", "sACT", M[("ssq2", qt)]["sACT"])
            pre2 = S.chain("vector", banks={0, 1}, fns=[
                lambda eng: eng.reduce_sum(st2[:, 0:1], PSf[:, 0:1024], axis=mybir.AxisListType.X),
                lambda eng: eng.tensor_scalar(st2[:, 2:3], st2[:, 0:1], 1.0 / 1024.0, None, ALU.mult),
                lambda eng: eng.tensor_tensor(st2[:, 3:4], st2[:, 2:3], st2[:, 2:3], ALU.mult),
            ])
            S.w("vector", "sDVE", pre2["sDVE"])
            M[("var2", qt)] = S.e("vector",
                                  lambda eng: eng.scalar_tensor_tensor(st2[:, 4:5], st2[:, 1:2], 1.0 / 1024.0,
                                                                       st2[:, 3:4], ALU.mult, ALU.subtract),
                                  ("sDVE",))
            S.w("scalar", "sDVE", M[("var2", qt)]["sDVE"])
            M[("rs2", qt)] = S.chain("scalar", [
                lambda eng: eng.activation(st2[:, 5:6], st2[:, 4:5], AF.Ln, bias=CE, scale=1.0),
                lambda eng: eng.activation(st2[:, 6:7], st2[:, 5:6], AF.Exp, bias=0.0, scale=CH),
            ])
            S.w("vector", "sACT", M[("rs2", qt)]["sACT"])
            M[("nb2", qt)] = S.e("vector",
                                 lambda eng: eng.scalar_tensor_tensor(st2[:, 7:8], st2[:, 2:3], -1.0,
                                                                      st2[:, 6:7], ALU.mult, ALU.mult),
                                 ("sDVE",))
            S.w("scalar", "sDVE", M[("nb2", qt)]["sDVE"])
            if qt > QT0:
                S.w("scalar", "sPOOL", M[("x2", qt - 1)]["sPOOL"])
            M[("t1", qt)] = S.e("scalar",
                                lambda eng: eng.activation(t1b[:], PSf[:, 0:1024], AF.Identity,
                                                           bias=st2[:, 7:8], scale=st2[:, 6:7]),
                                ("sACT",), banks={0, 1})

            S.w("gpsimd", "sACT", M[("t1", qt)]["sACT"])
            if qt >= QT0 + 2:
                psem = "sSPA" if (qt - 2) % 2 == 0 else "sSPB"
                S.w("gpsimd", psem, M[("spill", qt - 2)][psem])

            def x2_fn(qtt):
                def f(eng):
                    return eng.tensor_tensor(x2s[:, (qtt % 2) * 1024:(qtt % 2) * 1024 + 1024],
                                             x_tile(qtt), t1b[:], ALU.add)
                return f
            M[("x2", qt)] = S.e("gpsimd", x2_fn(qt), ("sPOOL",))
            S.w("sync", "sPOOL", M[("x2", qt)]["sPOOL"])
            spsem = "sSPA" if qt % 2 == 0 else "sSPB"
            M[("spill", qt)] = S.e("sync",
                                   (lambda qtt: (lambda eng: eng.dma_start(
                                       out=d_spill[(qtt - 3) * 128:(qtt - 2) * 128, :],
                                       in_=x2s[:, (qtt % 2) * 1024:(qtt % 2) * 1024 + 1024])))(qt),
                                   (spsem,), dma=True)

        emit_phase1()

        p1 = {s: S.c(s) for s in ("sPE", "sACT", "sDVE", "sPOOL")}

        def run_block():
            with nc.Block() as block:
                def runner(eng_obj, lst):
                    for item in lst:
                        if item[0] == "w":
                            eng_obj.wait_ge(sems[item[1]], item[2])
                        else:
                            inst = item[1](eng_obj)
                            for sem, amt in item[2]:
                                inst.then_inc(sems[sem], amt)

                @block.sync
                def _(e):
                    runner(e, S.ops["sync"])

                @block.tensor
                def _(e):
                    runner(e, S.ops["tensor"])

                @block.vector
                def _(e):
                    runner(e, S.ops["vector"])

                @block.scalar
                def _(e):
                    runner(e, S.ops["scalar"])

                @block.gpsimd
                def _(e):
                    runner(e, S.ops["gpsimd"])
            for k in ENGS:
                S.ops[k] = []

        run_block()
        es1.close()

        es2 = ExitStack()

        def sb2(name, shape, dt):
            return es2.enter_context(nc.sbuf_tensor(name, shape, dt))

        wr_b = sb2("wr_b", [128, 8 * 1024], BF16)
        wk_b = sb2("wk_b", [128, 8 * 1024], BF16)
        wv_b = sb2("wv_b", [128, 8 * 1024], BF16)
        stag2 = sb2("stag2", [128, 2 * 1024], F32)
        x2t = sb2("x2t", [128, 5 * 1024], F32)
        xl2 = sb2("xl2", [128, 2 * 1024], BF16)
        x2T = sb2("x2T", [128, 8 * 640], BF16)
        dbuf = sb2("dbuf", [128, 8 * 512], BF16)
        xrT = sb2("xrT", [128, 8 * 512], BF16)
        xkT = sb2("xkT", [128, 8 * 512], BF16)
        rbuf = sb2("rbuf", [128, 2 * 1024], F32)
        kbuf = sb2("kbuf", [128, 2 * 1024], BF16)
        kact = sb2("kact", [128, 2 * 1024], BF16)
        kactT = sb2("kactT", [128, 8 * 512], BF16)
        sgt = sb2("sgt", [128, 2 * 1024], F32)
        ffn = sb2("ffn", [128, 2 * 1024], F32)
        t2b = sb2("t2b", [128, 2 * 1024], F32)
        outb = sb2("outb", [128, 2 * 1024], F32)
        junk2 = sb2("junk2", [128, 1024], BF16)
        st3 = sb2("st3", [128, 16], F32)
        st4 = sb2("st4", [128, 16], F32)

        PS2 = es2.enter_context(nc.psum_tensor("PS2", [128, 4096], F32))
        PF = PS2.ap()
        PB2 = PS2.ap().bitcast(BF16)

        for eng in ENGS:
            for s in ("sPE", "sACT", "sDVE", "sPOOL"):
                S.w(eng, s, p1[s])

        for i in range(24):
            w = (d_wr, d_wk, d_wv)[i // 8]
            dstb = (wr_b, wk_b, wv_b)[i // 8]
            c = i % 8
            src = w[c * 128:(c + 1) * 128, :]
            dst = dstb[:, c * 1024:(c + 1) * 1024]
            slot = stag2[:, (i % 2) * 1024:(i % 2) * 1024 + 1024]
            wsem = "sWA" if i % 2 == 0 else "sWB"
            if i >= 2:
                S.w("sync", "sDVE", M[("cast2", i - 2)]["sDVE"])
            M[("wld2", i)] = S.e("sync",
                                 (lambda ss, hh: (lambda eng: eng.dma_start(out=ss, in_=hh)))(slot, src),
                                 (wsem,), dma=True)
            S.w("vector", wsem, M[("wld2", i)][wsem])
            M[("cast2", i)] = S.e("vector",
                                  (lambda ss, dd: (lambda eng: eng.tensor_copy(dd, ss)))(slot, dst),
                                  ("sDVE",))

        def st3c(tt, k):
            base = (tt % 2) * 8
            return st3[:, base + k:base + k + 1]

        def st4c(ot, k):
            base = (ot % 2) * 8
            return st4[:, base + k:base + k + 1]

        def emit_phase2():
            for sc in range(4):
                for tt in range(5):
                    gt = 4 * sc + tt
                    if sc > 0:
                        S.w("sync", "sDVE", M[("c2", sc - 1, tt)]["sDVE"])
                        if tt >= 1:
                            S.w("sync", "sPOOL", M[("ob", sc - 1, tt - 1)]["sPOOL"])
                    spsem = "sSPA" if (gt + 3) % 2 == 0 else "sSPB"
                    S.w("sync", spsem, M[("spill", gt + 3)][spsem])
                    if ("x2ld_prev",) in M:
                        S.w("sync", "sX2", M[("x2ld_prev",)]["sX2"])
                    M[("x2ld", sc, tt)] = S.e("sync",
                                              (lambda g, t: (lambda eng: eng.dma_start(
                                                  out=x2t[:, t * 1024:(t + 1) * 1024],
                                                  in_=d_spill[g * 128:(g + 1) * 128, :])))(gt, tt),
                                              ("sX2",), dma=True)
                    M[("x2ld_prev",)] = M[("x2ld", sc, tt)]
                    S.w("scalar", "sX2", M[("x2ld", sc, tt)]["sX2"])
                    M[("ssq3", sc, tt)] = S.e("scalar",
                                              (lambda t: (lambda eng: eng.activation(junk2[:], x2t[:, t * 1024:(t + 1) * 1024],
                                                                                     AF.Square, accum_out=st3c(t, 1))))(tt),
                                              ("sACT",))
                    S.w("vector", "sX2", M[("x2ld", sc, tt)]["sX2"])
                    pre3 = S.chain("vector", [
                        (lambda t: (lambda eng: eng.reduce_sum(st3c(t, 0), x2t[:, t * 1024:(t + 1) * 1024],
                                                               axis=mybir.AxisListType.X)))(tt),
                        (lambda t: (lambda eng: eng.tensor_scalar(st3c(t, 2), st3c(t, 0), 1.0 / 1024.0, None, ALU.mult)))(tt),
                        (lambda t: (lambda eng: eng.tensor_tensor(st3c(t, 3), st3c(t, 2), st3c(t, 2), ALU.mult)))(tt),
                    ])
                    S.w("vector", "sACT", M[("ssq3", sc, tt)]["sACT"])
                    S.w("vector", "sDVE", pre3["sDVE"])
                    M[("var3", sc, tt)] = S.e("vector",
                                              (lambda t: (lambda eng: eng.scalar_tensor_tensor(
                                                  st3c(t, 4), st3c(t, 1), 1.0 / 1024.0, st3c(t, 3),
                                                  ALU.mult, ALU.subtract)))(tt),
                                              ("sDVE",))
                    S.w("scalar", "sDVE", M[("var3", sc, tt)]["sDVE"])
                    M[("rs3", sc, tt)] = S.chain("scalar", [
                        (lambda t: (lambda eng: eng.activation(st3c(t, 5), st3c(t, 4), AF.Ln, bias=CE, scale=1.0)))(tt),
                        (lambda t: (lambda eng: eng.activation(st3c(t, 6), st3c(t, 5), AF.Exp, bias=0.0, scale=CH)))(tt),
                    ])
                    S.w("vector", "sACT", M[("rs3", sc, tt)]["sACT"])
                    M[("c2", sc, tt)] = S.chain("vector", [
                        (lambda t: (lambda eng: eng.scalar_tensor_tensor(st3c(t, 7), st3c(t, 2), -1.0, st3c(t, 6),
                                                                         ALU.mult, ALU.mult)))(tt),
                        (lambda t: (lambda eng: eng.tensor_scalar(xl2[:, (t % 2) * 1024:(t % 2) * 1024 + 1024],
                                                                  x2t[:, t * 1024:(t + 1) * 1024],
                                                                  st3c(t, 6), st3c(t, 7), ALU.mult, ALU.add)))(tt),
                    ])

                    S.w("tensor", "sDVE", M[("c2", sc, tt)]["sDVE"])
                    if (sc, tt) != (0, 0):
                        prev = (sc, tt - 1) if tt > 0 else (sc - 1, 4)
                        S.w("tensor", "sDVE", M[("e2", prev[0], prev[1])]["sDVE"])
                    if sc > 0 and tt == 0:
                        S.w("tensor", "sDVE", M[("ke", sc - 1, 3)]["sDVE"])

                    def t2_fn(t):
                        def f(eng):
                            r = None
                            for c in range(8):
                                r = eng.transpose(PB2[:, 6144 + c * 128:6272 + c * 128],
                                                  xl2[:, (t % 2) * 1024 + c * 128:(t % 2) * 1024 + (c + 1) * 128],
                                                  ident[:])
                            return r
                        return f
                    M[("pt2", sc, tt)] = S.e("tensor", t2_fn(tt), ("sPE",), banks={6})
                    S.w("vector", "sPE", M[("pt2", sc, tt)]["sPE"])

                    def e2_fn(t):
                        dst = bass.AP(x2T, t * 128, [[5120, 128], [640, 8], [1, 128]])

                        def f(eng):
                            return eng.tensor_copy(dst, PB2[:, 6144:7168].rearrange("p (c f) -> p c f", c=8))
                        return f
                    M[("e2", sc, tt)] = S.e("vector", e2_fn(tt), ("sDVE",), banks={6})

                if sc > 0:
                    S.w("vector", "sPE", M[("kmm", sc - 1, 3)]["sPE"])
                S.w("vector", "sDVE", M[("e2", sc, 4)]["sDVE"])

                def d_fn(eng):
                    src1 = bass.AP(x2T, 128, [[5120, 128], [640, 8], [1, 512]])
                    src0 = bass.AP(x2T, 127, [[5120, 128], [640, 8], [1, 512]])
                    dv = bass.AP(dbuf, 0, [[4096, 128], [512, 8], [1, 512]])
                    return eng.tensor_tensor(dv, src1, src0, ALU.subtract)

                def mix_fn(eng):
                    r = None
                    for c in range(8):
                        xp = bass.AP(x2T, c * 640 + 127, [[5120, 128], [1, 512]])
                        eng.scalar_tensor_tensor(xrT[:, c * 512:(c + 1) * 512],
                                                 dbuf[:, c * 512:(c + 1) * 512],
                                                 mus[:, c:c + 1], xp, ALU.mult, ALU.add)
                        r = eng.scalar_tensor_tensor(xkT[:, c * 512:(c + 1) * 512],
                                                     dbuf[:, c * 512:(c + 1) * 512],
                                                     mus[:, 8 + c:9 + c], xp, ALU.mult, ALU.add)
                    return r
                M[("mix", sc)] = S.chain("vector", [d_fn, mix_fn])

                for ot in range(4):
                    gt = 4 * sc + ot + 1
                    S.w("tensor", "sDVE", M[("mix", sc)]["sDVE"])
                    if sc + ot > 0:
                        prev = (sc, ot - 1) if ot > 0 else (sc - 1, 3)
                        S.w("tensor", "sACT", M[("relu", prev[0], prev[1])]["sACT"])
                    if ot == 0 and sc == 0:
                        S.w("tensor", "sDVE", M[("cast2", 23)]["sDVE"])

                    def rmm_fn(o):
                        def f(eng):
                            r = None
                            for c in range(8):
                                lhs = xrT[:, c * 512 + o * 128: c * 512 + o * 128 + 128]
                                for half in range(2):
                                    r = eng.matmul(PF[:, half * 512:half * 512 + 512], lhs,
                                                   wr_b[:, c * 1024 + half * 512: c * 1024 + half * 512 + 512],
                                                   start=(c == 0), stop=(c == 7))
                            return r
                        return f
                    M[("rmm", sc, ot)] = S.e("tensor", rmm_fn(ot), ("sPE",), banks={0, 1})

                    def kmm_fn(o):
                        def f(eng):
                            r = None
                            for c in range(8):
                                lhs = xkT[:, c * 512 + o * 128: c * 512 + o * 128 + 128]
                                for half in range(2):
                                    r = eng.matmul(PF[:, 1024 + half * 512:1024 + half * 512 + 512], lhs,
                                                   wk_b[:, c * 1024 + half * 512: c * 1024 + half * 512 + 512],
                                                   start=(c == 0), stop=(c == 7))
                            return r
                        return f
                    M[("kmm", sc, ot)] = S.e("tensor", kmm_fn(ot), ("sPE",), banks={2, 3})

                    S.w("scalar", "sPE", M[("rmm", sc, ot)]["sPE"])

                    def sigE_fn(o):
                        def f(eng):
                            return eng.activation(sgt[:, (o % 2) * 1024:(o % 2) * 1024 + 1024],
                                                  PF[:, 0:1024], AF.Exp, bias=0.0, scale=CN)
                        return f
                    M[("sigE", sc, ot)] = S.e("scalar", sigE_fn(ot), ("sACT",), banks={0, 1})
                    S.w("scalar", "sPE", M[("kmm", sc, ot)]["sPE"])

                    def relu_fn(o):
                        def f(eng):
                            return eng.activation(kbuf[:, (o % 2) * 1024:(o % 2) * 1024 + 1024],
                                                  PF[:, 1024:2048], AF.Relu)
                        return f
                    M[("relu", sc, ot)] = S.e("scalar", relu_fn(ot), ("sACT",), banks={2, 3})

                    S.w("vector", "sACT", M[("sigE", sc, ot)]["sACT"])

                    M[("sig", sc, ot)] = S.chain("vector", [
                        (lambda o: (lambda eng: eng.tensor_scalar(sgt[:, (o % 2) * 1024:(o % 2) * 1024 + 1024],
                                                                  sgt[:, (o % 2) * 1024:(o % 2) * 1024 + 1024],
                                                                  1.0, None, ALU.add)))(ot),
                        (lambda o: (lambda eng: eng.reciprocal(rbuf[:, (o % 2) * 1024:(o % 2) * 1024 + 1024],
                                                               sgt[:, (o % 2) * 1024:(o % 2) * 1024 + 1024])))(ot),
                    ])
                    S.w("vector", "sACT", M[("relu", sc, ot)]["sACT"])

                    def ka_fn(o):
                        def f(eng):
                            so = (o % 2) * 1024
                            return eng.tensor_tensor(kact[:, so:so + 1024], kbuf[:, so:so + 1024],
                                                     kbuf[:, so:so + 1024], ALU.mult)
                        return f
                    M[("ka", sc, ot)] = S.e("vector", ka_fn(ot), ("sDVE",))

                    S.w("tensor", "sDVE", M[("ka", sc, ot)]["sDVE"])
                    if sc + ot > 0:
                        prev = (sc, ot - 1) if ot > 0 else (sc - 1, 3)
                        S.w("tensor", "sDVE", M[("ke", prev[0], prev[1])]["sDVE"])

                    def tk2_fn(o):
                        def f(eng):
                            r = None
                            so = (o % 2) * 1024
                            for c in range(8):
                                r = eng.transpose(PB2[:, 7168 + c * 128:7296 + c * 128],
                                                  kact[:, so + c * 128: so + (c + 1) * 128], ident[:])
                            return r
                        return f
                    M[("ptk2", sc, ot)] = S.e("tensor", tk2_fn(ot), ("sPE",), banks={7})
                    S.w("vector", "sPE", M[("ptk2", sc, ot)]["sPE"])

                    def ke_fn(o):
                        dst = bass.AP(kactT, o * 128, [[4096, 128], [512, 8], [1, 128]])

                        def f(eng):
                            return eng.tensor_copy(dst, PB2[:, 7168:8192].rearrange("p (c f) -> p c f", c=8))
                        return f
                    M[("ke", sc, ot)] = S.e("vector", ke_fn(ot), ("sDVE",), banks={7})

                    S.w("tensor", "sDVE", M[("ke", sc, ot)]["sDVE"])
                    if sc + ot > 0:
                        prev = (sc, ot - 1) if ot > 0 else (sc - 1, 3)
                        S.w("tensor", "sDVE", M[("ff", prev[0], prev[1])]["sDVE"])

                    def vmm_fn(o):
                        def f(eng):
                            r = None
                            for c in range(8):
                                lhs = kactT[:, c * 512 + o * 128: c * 512 + o * 128 + 128]
                                for half in range(2):
                                    r = eng.matmul(PF[:, 2048 + half * 512:2048 + half * 512 + 512], lhs,
                                                   wv_b[:, c * 1024 + half * 512: c * 1024 + half * 512 + 512],
                                                   start=(c == 0), stop=(c == 7))
                            return r
                        return f
                    M[("vmm", sc, ot)] = S.e("tensor", vmm_fn(ot), ("sPE",), banks={4, 5})

                    S.w("vector", "sPE", M[("vmm", sc, ot)]["sPE"])
                    if sc * 4 + ot >= 2:
                        pr = (sc, ot - 2) if ot >= 2 else (sc - 1, 2 + ot)
                        S.w("vector", "sACT", M[("t2", pr[0], pr[1])]["sACT"])

                    S.w("vector", "sDVE", M[("sig", sc, ot)]["sDVE"])

                    def ff_fn(o):
                        def f(eng):
                            so = (o % 2) * 1024
                            return eng.tensor_tensor(ffn[:, so:so + 1024], PF[:, 2048:3072],
                                                     rbuf[:, so:so + 1024], ALU.mult)
                        return f
                    M[("ff", sc, ot)] = S.e("vector", ff_fn(ot), ("sDVE",), banks={4, 5})

                    S.w("scalar", "sDVE", M[("ff", sc, ot)]["sDVE"])

                    def ssq4_fn(o):
                        def f(eng):
                            so = (o % 2) * 1024
                            return eng.activation(junk2[:], ffn[:, so:so + 1024], AF.Square,
                                                  accum_out=st4c(o, 1))
                        return f
                    M[("ssq4", sc, ot)] = S.e("scalar", ssq4_fn(ot), ("sACT",))
                    S.w("vector", "sDVE", M[("ff", sc, ot)]["sDVE"])
                    pre4 = S.chain("vector", [
                        (lambda o: (lambda eng: eng.reduce_sum(st4c(o, 0),
                                                               ffn[:, (o % 2) * 1024:(o % 2) * 1024 + 1024],
                                                               axis=mybir.AxisListType.X)))(ot),
                        (lambda o: (lambda eng: eng.tensor_scalar(st4c(o, 2), st4c(o, 0), 1.0 / 1024.0, None, ALU.mult)))(ot),
                        (lambda o: (lambda eng: eng.tensor_tensor(st4c(o, 3), st4c(o, 2), st4c(o, 2), ALU.mult)))(ot),
                    ])
                    S.w("vector", "sACT", M[("ssq4", sc, ot)]["sACT"])
                    S.w("vector", "sDVE", pre4["sDVE"])
                    M[("var4", sc, ot)] = S.e("vector",
                                              (lambda o: (lambda eng: eng.scalar_tensor_tensor(
                                                  st4c(o, 4), st4c(o, 1), 1.0 / 1024.0, st4c(o, 3),
                                                  ALU.mult, ALU.subtract)))(ot),
                                              ("sDVE",))
                    S.w("scalar", "sDVE", M[("var4", sc, ot)]["sDVE"])
                    M[("rs4", sc, ot)] = S.chain("scalar", [
                        (lambda o: (lambda eng: eng.activation(st4c(o, 5), st4c(o, 4), AF.Ln, bias=CE, scale=1.0)))(ot),
                        (lambda o: (lambda eng: eng.activation(st4c(o, 6), st4c(o, 5), AF.Exp, bias=0.0, scale=CH)))(ot),
                    ])
                    S.w("vector", "sACT", M[("rs4", sc, ot)]["sACT"])
                    M[("nb4", sc, ot)] = S.e("vector",
                                             (lambda o: (lambda eng: eng.scalar_tensor_tensor(
                                                 st4c(o, 7), st4c(o, 2), -1.0, st4c(o, 6), ALU.mult, ALU.mult)))(ot),
                                             ("sDVE",))
                    S.w("scalar", "sDVE", M[("nb4", sc, ot)]["sDVE"])
                    if sc * 4 + ot >= 2:
                        pr = (sc, ot - 2) if ot >= 2 else (sc - 1, 2 + ot)
                        S.w("scalar", "sPOOL", M[("ob", pr[0], pr[1])]["sPOOL"])

                    def t2o_fn(o):
                        def f(eng):
                            so = (o % 2) * 1024
                            return eng.activation(t2b[:, so:so + 1024], ffn[:, so:so + 1024], AF.Identity,
                                                  bias=st4c(o, 7), scale=st4c(o, 6))
                        return f
                    M[("t2", sc, ot)] = S.e("scalar", t2o_fn(ot), ("sACT",))

                    S.w("gpsimd", "sACT", M[("t2", sc, ot)]["sACT"])
                    if sc * 4 + ot >= 2:
                        pr = (sc, ot - 2) if ot >= 2 else (sc - 1, 2 + ot)
                        stsem = "sSTA" if pr[1] % 2 == 0 else "sSTB"
                        S.w("gpsimd", stsem, M[("ost", pr[0], pr[1])][stsem])

                    def ob_fn(o):
                        def f(eng):
                            so = (o % 2) * 1024
                            return eng.tensor_tensor(outb[:, so:so + 1024],
                                                     x2t[:, (o + 1) * 1024:(o + 2) * 1024],
                                                     t2b[:, so:so + 1024], ALU.add)
                        return f
                    M[("ob", sc, ot)] = S.e("gpsimd", ob_fn(ot), ("sPOOL",))
                    S.w("sync", "sPOOL", M[("ob", sc, ot)]["sPOOL"])
                    grow = sc * 512 + ot * 128
                    stsem = "sSTA" if ot % 2 == 0 else "sSTB"
                    M[("ost", sc, ot)] = S.e("sync",
                                             (lambda g, o: (lambda eng: eng.dma_start(
                                                 out=d_out[g:g + 128, :],
                                                 in_=outb[:, (o % 2) * 1024:(o % 2) * 1024 + 1024])))(grow, ot),
                                             (stsem,), dma=True)

            S.w("sync", "sSTA", S.c("sSTA"))
            S.w("sync", "sSTB", S.c("sSTB"))

        if do_p2 and upto >= 99:
            emit_phase2()
        else:
            for sq in ("sSPA", "sSPB"):
                S.w("sync", sq, S.c(sq))
        run_block()
        es2.close()

        nc.all_engine_barrier()
        nc.gpsimd.dma_reset(sem_rng)
        nc.gpsimd.sem_clear(sem_rng)
        nc.all_engine_barrier()

    build_nc.last_sched = S
    return nc



def _prep_shared(inputs):
    w_qkv = np.ascontiguousarray(inputs["L0_attn_w_qkv"], dtype=np.float32)
    perm = np.zeros(D, dtype=np.int64)
    for h in range(NH):
        base = h * HD
        perm[base:base + 32] = base + np.arange(0, HD, 2)
        perm[base + 32:base + 64] = base + np.arange(1, HD, 2)
    wq = w_qkv[:, 0:D][:, perm]
    wk = w_qkv[:, D:2 * D][:, perm]
    wv = w_qkv[:, 2 * D:3 * D]
    wqkv_p = np.ascontiguousarray(np.concatenate([wq, wk, wv], axis=1))
    wo_raw = inputs["L0_attn_w_out"].astype(np.float32)
    wout = np.ascontiguousarray(wo_raw.reshape(16, 64, D).transpose(1, 0, 2).reshape(64, 16 * D))
    wr = np.ascontiguousarray(inputs["L0_ffn_W_r"].T, dtype=np.float32)
    wkf = np.ascontiguousarray(inputs["L0_ffn_W_k"].T, dtype=np.float32)
    wvf = np.ascontiguousarray(inputs["L0_ffn_W_v"].T, dtype=np.float32)
    mus = np.zeros((128, 16), np.float32)
    mus[:, 0:8] = inputs["L0_ffn_mu_r"].astype(np.float32).reshape(8, 128).T
    mus[:, 8:16] = inputs["L0_ffn_mu_k"].astype(np.float32).reshape(8, 128).T
    consts = np.zeros((128, 8), np.float32)
    consts[:, 0] = -BEXP
    consts[:, 1] = 0.125
    consts[:, 2] = EPS
    consts[:, 3] = -0.5
    consts[:, 4] = -1.0
    ident = np.eye(128, dtype=bfloat16)
    ii = np.arange(128)
    mfirst = np.where(ii[None, :] >= ii[:, None] + 1, 0.0, MASKV).astype(bfloat16)
    mlast = np.where(ii[None, :] <= ii[:, None], 0.0, MASKV).astype(bfloat16)
    return dict(wqkv=wqkv_p, wout=wout, wr=wr, wk=wkf, wv=wvf, mus=mus,
                consts=consts, ident=ident, mfirst=mfirst, mlast=mlast)


def _tabs_for(pos):
    p = np.maximum(pos, 0).astype(np.float64)
    inv_freq = 1.0 / 10000.0 ** (np.arange(0, HD, 2, dtype=np.float64) / HD)
    fr = p[:, None] * inv_freq[None, :]
    cos = np.cos(fr)
    sin = np.sin(fr)
    t = np.zeros((EXT, 128), np.float32)
    t[:, 0:32] = cos
    t[:, 32:64] = cos
    t[:, 64:96] = -sin
    t[:, 96:128] = sin
    return t


def _tabs_sbufmajor(t):
    out = np.zeros((128, EXT), np.float32)
    for j in range(NT):
        out[:, j * 128:(j + 1) * 128] = t[j * 128:(j + 1) * 128, :]
    return out


def _mq_for(half):
    ii = np.arange(128)
    mfirst = np.where(ii[None, :] >= ii[:, None] + 1, 0.0, MASKV).astype(np.float32)
    out = np.zeros((128, 2048), np.float32)
    for q in range(4):
        qt = 4 + q
        blk = np.zeros((128, 512), np.float32)
        if half == 1:
            blk[:, 0:128] = mfirst
        else:
            for ki in range(4):
                ka = qt - 4 + ki
                if ka < 4:
                    blk[:, ki * 128:(ki + 1) * 128] = MASKV
        out[:, q * 512:(q + 1) * 512] = blk
    return out.astype(bfloat16)


def kernel(L0_attn_x, L0_attn_w_qkv, L0_attn_w_out, L0_ffn_W_r, L0_ffn_W_k,
           L0_ffn_W_v, L0_ffn_mu_r, L0_ffn_mu_k):
    inputs = dict(L0_attn_x=np.asarray(L0_attn_x), L0_attn_w_qkv=np.asarray(L0_attn_w_qkv),
                  L0_attn_w_out=np.asarray(L0_attn_w_out), L0_ffn_W_r=np.asarray(L0_ffn_W_r),
                  L0_ffn_W_k=np.asarray(L0_ffn_W_k), L0_ffn_W_v=np.asarray(L0_ffn_W_v),
                  L0_ffn_mu_r=np.asarray(L0_ffn_mu_r), L0_ffn_mu_k=np.asarray(L0_ffn_mu_k))
    sh = _prep_shared(inputs)
    x = inputs["L0_attn_x"].astype(np.float32)

    tabs_e = _tabs_sbufmajor(_tabs_for(np.arange(-512, 2048)))
    tabs_o = _tabs_sbufmajor(_tabs_for(np.arange(1536, 4096)))
    mq_e = _mq_for(0)
    mq_o = _mq_for(1)

    in_maps = []
    for core in range(8):
        b, half = core // 2, core % 2
        if half == 0:
            x_ext = np.concatenate([np.zeros((512, D), np.float32), x[b, 0:2048]], axis=0)
        else:
            x_ext = np.ascontiguousarray(x[b, 1536:4096])
        in_maps.append({
            "x_ext": x_ext,
            "wqkv": sh["wqkv"], "wout": sh["wout"],
            "wr": sh["wr"], "wk": sh["wk"], "wv": sh["wv"],
            "tabs": (tabs_o if half else tabs_e),
            "mus": sh["mus"], "consts": sh["consts"],
            "ident": sh["ident"],
            "mfirst": sh["mfirst"], "mlast": sh["mlast"],
            "mq": (mq_o if half else mq_e),
        })

    nc = build_nc()
    res = run_bass_kernel_spmd(nc, in_maps, core_ids=list(range(8)))
    out = np.zeros((4, 4096, D), np.float32)
    for core in range(8):
        b, half = core // 2, core % 2
        out[b, half * 2048:(half + 1) * 2048] = res.results[core]["out"]
    return out


# version 07 (final)
# speedup vs baseline: 1.0487x; 1.0465x over previous
import math
import numpy as np
import ml_dtypes

import concourse.bass as bass
import concourse.mybir as mybir
from concourse.bass_utils import run_bass_kernel_spmd
from contextlib import ExitStack

F32 = mybir.dt.float32
BF16 = mybir.dt.bfloat16
AF = mybir.ActivationFunctionType
ALU = mybir.AluOpType
bfloat16 = ml_dtypes.bfloat16

D = 1024
NH = 16
HD = 64
EXT = 2560
NT = 20
QT0 = 3
NQT = 17
OUT_T = 2048
BEXP = 16.0
MASKV = -30000.0
EPS = 1e-5

ENGS = ("sync", "tensor", "vector", "scalar", "gpsimd")


class Sched:

    def __init__(self):
        self.cnt = {}
        self.ops = {e: [] for e in ENGS}
        self.trace = {e: [] for e in ENGS}
        self.label = None

    def w(self, eng, sem, val):
        if val > 0:
            self.ops[eng].append(("w", sem, val))
            self.trace[eng].append(("w", sem, val, self.label, frozenset()))

    def e(self, eng, fn, incs=(), dma=False, banks=()):
        out = {}
        amt = 16 if dma else 1
        incl = []
        for sem in incs:
            self.cnt[sem] = self.cnt.get(sem, 0) + amt
            incl.append((sem, amt))
            out[sem] = self.cnt[sem]
        self.ops[eng].append(("e", fn, incl))
        self.trace[eng].append(("e", incl, self.label, frozenset(banks)))
        return out

    def c(self, sem):
        return self.cnt.get(sem, 0)

    def chain(self, eng, fns, incs=(), banks=()):
        sem = {"vector": "sDVE", "scalar": "sACT", "tensor": "sPE",
               "gpsimd": "sPOOL"}[eng]

        def wrapper(eng_obj, _fns=tuple(fns)):
            r = None
            for i, fn in enumerate(_fns):
                if i:
                    eng_obj.drain()
                r = fn(eng_obj)
            return r
        want = set(incs) | {sem}
        return self.e(eng, wrapper, tuple(sorted(want)), banks=banks)


def build_nc(nt=NT, do_attn=True, do_p2=True, upto=99):
    nc = bass.Bass("TRN2", target_bir_lowering=False)

    d_x = nc.dram_tensor("x_ext", [EXT, D], F32, kind="ExternalInput").ap()
    d_wqkv = nc.dram_tensor("wqkv", [D, 3 * D], F32, kind="ExternalInput").ap()
    d_wout = nc.dram_tensor("wout", [64, 16 * 1024], F32, kind="ExternalInput").ap()
    d_wr = nc.dram_tensor("wr", [D, D], F32, kind="ExternalInput").ap()
    d_wk = nc.dram_tensor("wk", [D, D], F32, kind="ExternalInput").ap()
    d_wv = nc.dram_tensor("wv", [D, D], F32, kind="ExternalInput").ap()
    d_tabs = nc.dram_tensor("tabs", [128, EXT], F32, kind="ExternalInput").ap()
    d_mus = nc.dram_tensor("mus", [128, 16], F32, kind="ExternalInput").ap()
    d_consts = nc.dram_tensor("consts", [128, 8], F32, kind="ExternalInput").ap()
    d_ident = nc.dram_tensor("ident", [128, 128], BF16, kind="ExternalInput").ap()
    d_mfirst = nc.dram_tensor("mfirst", [128, 128], BF16, kind="ExternalInput").ap()
    d_mlast = nc.dram_tensor("mlast", [128, 128], BF16, kind="ExternalInput").ap()
    d_mq = nc.dram_tensor("mq", [128, 2048], BF16, kind="ExternalInput").ap()
    d_out = nc.dram_tensor("out", [OUT_T, D], F32, kind="ExternalOutput").ap()
    d_spill = nc.dram_tensor("x2spill", [17 * 128, D], F32, kind="Internal").ap()

    es = ExitStack()

    def sb(name, shape, dt):
        return es.enter_context(nc.sbuf_tensor(name, shape, dt))

    consts = sb("sb_consts", [128, 8], F32)
    mus = sb("sb_mus", [128, 16], F32)
    ident = sb("sb_ident", [128, 128], BF16)

    sems = {}
    for s in ("sSET", "sWA", "sWB", "sXA", "sXB", "sSPA", "sSPB", "sX2",
              "sSTA", "sSTB", "sPE", "sACT", "sDVE", "sPOOL"):
        sems[s] = es.enter_context(nc.semaphore(s))

    sem_nums = sorted(h.num for h in sems.values())
    sem_rng = range(sem_nums[0], sem_nums[-1] + 1)
    assert len(sem_nums) == len(sem_rng), "semaphore numbers not contiguous"
    nc.all_engine_barrier()
    nc.gpsimd.dma_reset(sem_rng)
    nc.gpsimd.sem_clear(sem_rng)
    nc.all_engine_barrier()

    S = Sched()
    M = {}

    CB = consts[:, 0:1]
    CS = consts[:, 1:2]
    CE = consts[:, 2:3]
    CH = consts[:, 3:4]
    CN = consts[:, 4:5]

    with es:
        es1 = ExitStack()

        def sb1(name, shape, dt):
            return es1.enter_context(nc.sbuf_tensor(name, shape, dt))

        wqkv_b = sb1("wqkv_b", [128, 8 * 3072], BF16)
        wout_b = sb1("wout_b", [64, 16 * 1024], BF16)
        stag = sb1("stag", [128, 2 * 1024], F32)
        kT_ring = sb1("kT_ring", [128, 8 * 1024], BF16)
        v_ring = sb1("v_ring", [128, 8 * 1024], BF16)
        qTp = sb1("qTp", [128, 2 * 2048], BF16)
        xb = sb1("xb", [128, 2 * 1024], F32)
        xln1 = sb1("xln1", [128, 1024], BF16)
        xlt = sb1("xlt", [128, 1024], BF16)
        tabs = sb1("sb_tabs", [128, 2560], F32)
        mfirst = sb1("sb_mfirst", [128, 128], BF16)
        mlast = sb1("sb_mlast", [128, 128], BF16)
        mq = sb1("sb_mq", [128, 2048], BF16)
        tmpA = sb1("tmpA", [128, 2048], F32)
        tmpB = sb1("tmpB", [128, 2048], F32)
        qnat = sb1("qnat", [128, 1024], BF16)
        knat = sb1("knat", [128, 1024], BF16)
        Pb = sb1("Pb", [128, 2 * 1280], BF16)
        PTb = sb1("PTb", [128, 2 * 1280], BF16)
        oT = sb1("oT", [128, 2048], BF16)
        sums = sb1("sums", [128, 128], F32)
        t1b = sb1("t1b", [128, 1024], F32)
        x2s = sb1("x2s", [128, 2 * 1024], F32)
        junk = sb1("junk", [128, 1024], BF16)
        st1 = sb1("st1", [128, 32], F32)
        st2 = sb1("st2", [128, 16], F32)

        PS = es1.enter_context(nc.psum_tensor("PS", [128, 4096], F32))
        PSf = PS.ap()
        PSb = PS.ap().bitcast(BF16)

        setup_ld = None
        for t, h in ((consts, d_consts), (mus, d_mus), (ident, d_ident),
                     (tabs, d_tabs), (mfirst, d_mfirst), (mlast, d_mlast), (mq, d_mq)):
            setup_ld = S.e("sync", (lambda tt, hh: (lambda eng: eng.dma_start(out=tt[:], in_=hh)))(t, h),
                           ("sSET",), dma=True)
        for eng in ("tensor", "vector", "scalar", "gpsimd"):
            S.w(eng, "sSET", setup_ld["sSET"])
        S.e("vector", lambda eng: eng.memset(qTp[:], 0.0), ("sDVE",))

        witems = []
        for c in range(8):
            for hh in range(3):
                witems.append((d_wqkv[c * 128:(c + 1) * 128, hh * 1024:(hh + 1) * 1024],
                               wqkv_b[:, c * 3072 + hh * 1024:c * 3072 + (hh + 1) * 1024], 128))
        for c in range(16):
            witems.append((d_wout[:, c * 1024:(c + 1) * 1024],
                           wout_b[0:64, c * 1024:(c + 1) * 1024], 64))
        for i, (src, dst, nparts) in enumerate(witems):
            slot = stag[0:nparts, (i % 2) * 1024:(i % 2) * 1024 + 1024]
            wsem = "sWA" if i % 2 == 0 else "sWB"
            if i >= 2:
                S.w("sync", "sDVE", M[("cast", i - 2)]["sDVE"])
            M[("wld", i)] = S.e("sync",
                                (lambda ss, hh: (lambda eng: eng.dma_start(out=ss, in_=hh)))(slot, src),
                                (wsem,), dma=True)
            S.w("vector", wsem, M[("wld", i)][wsem])
            M[("cast", i)] = S.e("vector",
                                 (lambda ss, dd: (lambda eng: eng.tensor_copy(dd, ss)))(slot, dst),
                                 ("sDVE",))

        def x_tile(j):
            return xb[:, (j % 2) * 1024:(j % 2) * 1024 + 1024]

        def st1c(j, k):
            base = (j % 2) * 8
            return st1[:, base + k:base + k + 1]

        def emit_phase1():
            for j in range(nt):
                qt = j if j >= QT0 else None
                rot = j % 2

                if j >= 2:
                    if ("x2", j - 2) in M:
                        S.w("sync", "sPOOL", M[("x2", j - 2)]["sPOOL"])
                    else:
                        S.w("sync", "sDVE", M[("c1", j - 2)]["sDVE"])
                        S.w("sync", "sACT", M[("ssq", j - 2)]["sACT"])
                xsem = "sXA" if j % 2 == 0 else "sXB"
                M[("xld", j)] = S.e("sync",
                                    (lambda jj: (lambda eng: eng.dma_start(out=x_tile(jj), in_=d_x[jj * 128:(jj + 1) * 128, :])))(j),
                                    (xsem,), dma=True)

                S.w("scalar", xsem, M[("xld", j)][xsem])
                M[("ssq", j)] = S.e("scalar",
                                    (lambda jj: (lambda eng: eng.activation(junk[:], x_tile(jj), AF.Square,
                                                                            accum_out=st1c(jj, 1))))(j),
                                    ("sACT",))
                S.w("vector", xsem, M[("xld", j)][xsem])
                pre = S.chain("vector", [
                    (lambda jj: (lambda eng: eng.reduce_sum(st1c(jj, 0), x_tile(jj), axis=mybir.AxisListType.X)))(j),
                    (lambda jj: (lambda eng: eng.tensor_scalar(st1c(jj, 2), st1c(jj, 0), 1.0 / 1024.0, None, ALU.mult)))(j),
                    (lambda jj: (lambda eng: eng.tensor_tensor(st1c(jj, 3), st1c(jj, 2), st1c(jj, 2), ALU.mult)))(j),
                ])
                S.w("vector", "sACT", M[("ssq", j)]["sACT"])
                S.w("vector", "sDVE", pre["sDVE"])
                M[("var", j)] = S.e("vector",
                                    (lambda jj: (lambda eng: eng.scalar_tensor_tensor(st1c(jj, 4), st1c(jj, 1), 1.0 / 1024.0,
                                                                                      st1c(jj, 3), ALU.mult, ALU.subtract)))(j),
                                    ("sDVE",))
                S.w("scalar", "sDVE", M[("var", j)]["sDVE"])
                M[("rs1", j)] = S.chain("scalar", [
                    (lambda jj: (lambda eng: eng.activation(st1c(jj, 5), st1c(jj, 4), AF.Ln, bias=CE, scale=1.0)))(j),
                    (lambda jj: (lambda eng: eng.activation(st1c(jj, 6), st1c(jj, 5), AF.Exp, bias=0.0, scale=CH)))(j),
                ])
                S.w("vector", "sACT", M[("rs1", j)]["sACT"])
                M[("c1", j)] = S.chain("vector", [
                    (lambda jj: (lambda eng: eng.scalar_tensor_tensor(st1c(jj, 7), st1c(jj, 2), -1.0, st1c(jj, 6),
                                                                      ALU.mult, ALU.mult)))(j),
                    (lambda jj: (lambda eng: eng.tensor_scalar(xln1[:], x_tile(jj), st1c(jj, 6), st1c(jj, 7),
                                                               ALU.mult, ALU.add)))(j),
                ])

                S.w("tensor", "sDVE", M[("c1", j)]["sDVE"])
                if j > 0:
                    S.w("tensor", "sDVE", M[("ktev", j - 1)]["sDVE"])
                if ("otev", j - 1, 7) in M:
                    S.w("tensor", "sDVE", M[("otev", j - 1, 7)]["sDVE"])

                def t1_fn(jj):
                    def f(eng):
                        r = None
                        for c in range(8):
                            r = eng.transpose(PSb[:, 6144 + c * 128:6272 + c * 128],
                                              xln1[:, c * 128:(c + 1) * 128], ident[:])
                        return r
                    return f
                M[("pt1", j)] = S.e("tensor", t1_fn(j), ("sPE",), banks={6})

                S.w("vector", "sPE", M[("pt1", j)]["sPE"])
                M[("e1", j)] = S.e("vector", lambda eng: eng.tensor_copy(xlt[:], PSb[:, 6144:7168]), ("sDVE",), banks={6})

                S.w("tensor", "sDVE", M[("e1", j)]["sDVE"])
                if j == 0:
                    S.w("tensor", "sDVE", M[("cast", 39)]["sDVE"])
                if ("t1", j - 1) in M:
                    S.w("tensor", "sACT", M[("t1", j - 1)]["sACT"])
                    S.w("tensor", "sDVE", M[("nb2", j - 1)]["sDVE"])
                elif j > 0:
                    S.w("tensor", "sACT", M[("vev", j - 1)]["sACT"])
                    if ("ropeK", j - 1, "d") in M:
                        S.w("tensor", "sDVE", M[("ropeK", j - 1, "d")]["sDVE"])

                def qkv_part(jj, w):
                    def f(eng):
                        r = None
                        for c in range(8):
                            lhs = xlt[:, c * 128:(c + 1) * 128]
                            for half in range(2):
                                r = eng.matmul(PSf[:, w * 1024 + half * 512: w * 1024 + half * 512 + 512],
                                               lhs,
                                               wqkv_b[:, c * 3072 + w * 1024 + half * 512: c * 3072 + w * 1024 + half * 512 + 512],
                                               start=(c == 0), stop=(c == 7))
                        return r
                    return f
                M[("qkvq", j)] = S.e("tensor", qkv_part(j, 0), ("sPE",), banks={0, 1})
                M[("qkvk", j)] = S.e("tensor", qkv_part(j, 1), ("sPE",), banks={2, 3})
                M[("qkv", j)] = S.e("tensor", qkv_part(j, 2), ("sPE",), banks={4, 5})

                S.w("scalar", "sPE", M[("qkv", j)]["sPE"])
                M[("vev", j)] = S.e("scalar",
                                    (lambda jj: (lambda eng: eng.activation(v_ring[:, (jj % 8) * 1024:(jj % 8) * 1024 + 1024],
                                                                            PSf[:, 2048:3072], AF.Copy)))(j),
                                    ("sACT",), banks={4, 5})

                def rope_fn(jj, base_off, toff):
                    cos_ap = bass.AP(tabs, jj * 128, [[2560, 128], [0, 16], [0, 2], [1, 32]])
                    sin_ap = bass.AP(tabs, jj * 128 + 64, [[2560, 128], [0, 16], [32, 2], [1, 32]])
                    plain = bass.AP(PS, base_off, [[4096, 128], [64, 16], [32, 2], [1, 32]])
                    swap = bass.AP(PS, base_off + 32, [[4096, 128], [64, 16], [-32, 2], [1, 32]])
                    tA = bass.AP(tmpA, toff, [[2048, 128], [64, 16], [32, 2], [1, 32]])
                    tB = bass.AP(tmpB, toff, [[2048, 128], [64, 16], [32, 2], [1, 32]])

                    def f(eng):
                        eng.tensor_tensor(tA, plain, cos_ap, ALU.mult)
                        return eng.tensor_tensor(tB, swap, sin_ap, ALU.mult)
                    return f

                if j >= QT0 and upto != 21:
                    S.w("vector", "sPE", M[("qkvq", j)]["sPE"])
                else:
                    S.w("vector", "sPE", M[("qkvk", j)]["sPE"])
                if upto == 21:
                    M[("ropeK", j, "g")] = M[("qkv", j)]
                    M[("ktev", j)] = M[("c1", j)]
                    M[("qtev", j)] = M[("c1", j)]
                if j >= QT0 and upto != 21:
                    M[("ropeQ", j, "g")] = S.chain("vector", [
                        rope_fn(j, 0, 0),
                        lambda eng: eng.tensor_tensor(qnat[:], tmpA[:, 0:1024], tmpB[:, 0:1024], ALU.add),
                    ], banks={0, 1})
                    M[("ropeQ", j, "d")] = M[("ropeQ", j, "g")]

                if upto != 21:
                    S.w("vector", "sPE", M[("qkvk", j)]["sPE"])
                    M[("ropeK", j, "g")] = S.chain("vector", [
                        rope_fn(j, 1024, 1024),
                        lambda eng: eng.tensor_tensor(knat[:], tmpA[:, 1024:2048], tmpB[:, 1024:2048], ALU.add),
                    ], banks={2, 3})
                    M[("ropeK", j, "d")] = M[("ropeK", j, "g")]

                if j >= QT0 and upto != 21:
                    S.w("tensor", "sDVE", M[("ropeQ", j, "g")]["sDVE"])
                    if j > QT0:
                        S.w("tensor", "sDVE", M[("qtev", j - 1)]["sDVE"])

                    def tq_fn(jj):
                        def f(eng):
                            r = None
                            for c in range(8):
                                r = eng.transpose(PSb[:, 7168 + c * 128:7296 + c * 128],
                                                  qnat[:, c * 128:(c + 1) * 128], ident[:])
                            return r
                        return f
                    M[("ptq", j)] = S.e("tensor", tq_fn(j), ("sPE",), banks={7})
                    S.w("vector", "sPE", M[("ptq", j)]["sPE"])
                    def qtev_fn(jj):
                        rot = jj % 2
                        def f(eng):
                            srcA = bass.AP(PS, 3584, [[4096, 64], [64, 8], [1, 64]]).bitcast(BF16) if False else None
                            sA = bass.AP(PSb.tensor if hasattr(PSb, "tensor") else PS, 0, [])
                            a_src = bass.AP(PS, 3584, [[2048, 64], [64, 8], [1, 64]])
                            return eng.tensor_copy(
                                bass.AP(qTp, rot * 2048, [[4096, 64], [256, 8], [1, 128]]),
                                a_src.bitcast(BF16))
                        return f
                    def qtev2_fn(jj):
                        rot = jj % 2
                        def f(eng):
                            eng.tensor_copy(
                                bass.AP(qTp, rot * 2048, [[4096, 64], [256, 8], [1, 128]]),
                                bass.AP(qTp, 0, []) if False else
                                PSb[0:64, 7168:8192].rearrange("p (c f) -> p c f", c=8))
                            return eng.tensor_copy(
                                bass.AP(qTp, 4096 * 64 + rot * 2048 + 128, [[4096, 64], [256, 8], [1, 128]]),
                                PSb[64:128, 7168:8192].rearrange("p (c f) -> p c f", c=8))
                        return f
                    M[("qtev", j)] = S.e("vector", qtev2_fn(j), ("sDVE",), banks={7})

                if upto != 21:
                    S.w("tensor", "sDVE", M[("ropeK", j, "g")]["sDVE"])
                S.w("tensor", "sDVE", M[("e1", j)]["sDVE"])

                if upto != 21:
                    def tk_fn(jj):
                        def f(eng):
                            r = None
                            for c in range(8):
                                r = eng.transpose(PSb[:, 6144 + c * 128:6272 + c * 128],
                                                  knat[:, c * 128:(c + 1) * 128], ident[:])
                            return r
                        return f
                    M[("ptk", j)] = S.e("tensor", tk_fn(j), ("sPE",), banks={6})
                    S.w("vector", "sPE", M[("ptk", j)]["sPE"])
                    M[("ktev", j)] = S.e("vector",
                                         (lambda jj: (lambda eng: eng.tensor_copy(kT_ring[:, (jj % 8) * 1024:(jj % 8) * 1024 + 1024],
                                                                                  PSb[:, 6144:7168])))(j),
                                         ("sDVE",), banks={6})

                if qt is not None and do_attn:
                    emit_attn(qt, upto)

        def emit_attn(qt, upto=99):
            j = qt
            kts = list(range(max(0, qt - 4), qt))
            nkt = len(kts)
            qclass = qt - 4 if 4 <= qt <= 7 else None

            for hp in range(8):
                if False:
                    for ss in ("sDVE", "sACT", "sPOOL"):
                        S.w("tensor", ss, S.c(ss))
                    for ss in ("sPE", "sACT"):
                        S.w("vector", ss, S.c(ss))
                    for ss in ("sPE", "sDVE"):
                        S.w("scalar", ss, S.c(ss))
                rot = hp % 2
                soff = rot * 1536
                S.w("tensor", "sDVE", M[("qtev", qt)]["sDVE"])
                S.w("tensor", "sDVE", M[("ktev", qt)]["sDVE"])
                if hp == 1:
                    S.w("tensor", "sACT", M[("vev", qt)]["sACT"])
                for key in ((qt, hp - 2) if hp >= 2 else (qt - 1, 6 + hp),):
                    for stage, sem in (("exp", "sACT"), ("ptev", "sDVE"), ("nm", "sDVE"),
                                       ("mk", "sDVE")):
                        if (stage, key[0], key[1]) in M:
                            S.w("tensor", sem, M[(stage, key[0], key[1])][sem])
                            if stage in ("exp",):
                                continue
                            break
                    else:
                        continue

                def sc_fn(qtt, hpp, so):
                    def f(eng):
                        r = None
                        for hh in range(2):
                            h = 2 * hpp + hh
                            lhs = qTp[:, (qtt % 2) * 2048 + h * 128:(qtt % 2) * 2048 + (h + 1) * 128]
                            hoff = so + hh * 640
                            for ki, ka in enumerate(kts):
                                slot = ka % 8
                                rhs = kT_ring[:, slot * 1024 + hpp * 128: slot * 1024 + hpp * 128 + 128]
                                r = eng.matmul(PSf[:, hoff + ki * 128: hoff + ki * 128 + 128],
                                               lhs, rhs, start=True, stop=True)
                            slot = qtt % 8
                            rhs = kT_ring[:, slot * 1024 + hpp * 128: slot * 1024 + hpp * 128 + 128]
                            r = eng.matmul(PSf[:, hoff + 512: hoff + 640],
                                           lhs, rhs, start=True, stop=True)
                        return r
                    return f
                M[("sc", qt, hp)] = S.e("tensor", sc_fn(qt, hp, soff), ("sPE",), banks={3 * rot, 3 * rot + 1, 3 * rot + 2})

                if upto in (13, 15) and hp > 0:
                    break
                if upto < 2 or upto in (11, 13, 15):
                    continue
                S.w("vector", "sPE", M[("sc", qt, hp)]["sPE"])

                def mk_fn(qtt, hpp, so):
                    def f(eng):
                        if upto == 12:
                            return eng.tensor_copy(junk[:, 0:128], xln1[:, 0:128])
                        r = None
                        if qclass is not None:
                            src = bass.AP(PS, so, [[4096, 128], [640, 2], [1, 512]])
                            m = bass.AP(mq, qclass * 512, [[2048, 128], [0, 2], [1, 512]])
                            r = eng.tensor_tensor(src, src, m, ALU.add)
                        elif qtt >= 8:
                            src = bass.AP(PS, so, [[4096, 128], [640, 2], [1, 128]])
                            m = bass.AP(mfirst, 0, [[128, 128], [0, 2], [1, 128]])
                            r = eng.tensor_tensor(src, src, m, ALU.add)
                        srcd = bass.AP(PS, so + 512, [[4096, 128], [640, 2], [1, 128]])
                        md = bass.AP(mlast, 0, [[128, 128], [0, 2], [1, 128]])
                        return eng.tensor_tensor(srcd, srcd, md, ALU.add)
                    return f
                M[("mk", qt, hp)] = S.e("vector", mk_fn(qt, hp, soff), ("sDVE",), banks={3 * rot, 3 * rot + 1, 3 * rot + 2})

                if upto < 3:
                    continue
                S.w("scalar", "sDVE", M[("mk", qt, hp)]["sDVE"])

                def exp_fn(qtt, hpp, so):
                    srot = (qtt % 2) * 64
                    def f(eng):
                        r = None
                        for hh in range(2):
                            h = 2 * hpp + hh
                            hoff = so + hh * 640
                            poff = (hpp % 2) * 1280 + hh * 640
                            if qtt == QT0:
                                eng.activation(Pb[:, poff:poff + 384], PSf[:, hoff:hoff + 384],
                                               AF.Exp, bias=CB, scale=CS,
                                               accum_out=sums[:, srot + h:srot + h + 1])
                                r = eng.activation(Pb[:, poff + 512:poff + 640], PSf[:, hoff + 512:hoff + 640],
                                                   AF.Exp, bias=CB, scale=CS,
                                                   accum_out=sums[:, srot + 16 + h:srot + 16 + h + 1])
                            else:
                                r = eng.activation(Pb[:, poff:poff + 640], PSf[:, hoff:hoff + 640],
                                                   AF.Exp, bias=CB, scale=CS,
                                                   accum_out=sums[:, srot + h:srot + h + 1])
                        return r
                    return f
                M[("exp", qt, hp)] = S.e("scalar", exp_fn(qt, hp, soff), ("sACT",), banks={3 * rot, 3 * rot + 1, 3 * rot + 2})

                if upto < 4:
                    continue
                S.w("vector", "sACT", M[("exp", qt, hp)]["sACT"])
                srot_ = (qt % 2) * 64
                h0_ = 2 * hp
                pre_ops = []
                if qt == QT0:
                    pre_ops.append((lambda sr, h0: (lambda eng: eng.tensor_tensor(
                        sums[:, sr + h0:sr + h0 + 2], sums[:, sr + h0:sr + h0 + 2],
                        sums[:, sr + 16 + h0:sr + 16 + h0 + 2], ALU.add)))(srot_, h0_))
                pre_ops.append((lambda sr, h0: (lambda eng: eng.reciprocal(
                    sums[:, sr + 32 + h0:sr + 32 + h0 + 2],
                    sums[:, sr + h0:sr + h0 + 2])))(srot_, h0_))

                def nm_fn(qtt, hpp):
                    srot = (qtt % 2) * 64
                    h0 = 2 * hpp
                    def f(eng):
                        r = None
                        for hh in range(2):
                            poff = (hpp % 2) * 1280 + hh * 640
                            inv = sums[:, srot + 32 + h0 + hh:srot + 33 + h0 + hh]
                            if qtt == QT0:
                                eng.tensor_scalar(Pb[:, poff:poff + 384], Pb[:, poff:poff + 384],
                                                  inv, None, ALU.mult)
                                r = eng.tensor_scalar(Pb[:, poff + 512:poff + 640], Pb[:, poff + 512:poff + 640],
                                                      inv, None, ALU.mult)
                            else:
                                r = eng.tensor_scalar(Pb[:, poff:poff + 640], Pb[:, poff:poff + 640],
                                                      inv, None, ALU.mult)
                        return r
                    return f
                M[("nm", qt, hp)] = S.chain("vector", pre_ops + [nm_fn(qt, hp)])

                if upto < 5:
                    continue
                S.w("tensor", "sDVE", M[("nm", qt, hp)]["sDVE"])
                pdst = rot * 3072

                def pt_fn(qtt, hpp):
                    prot = (hpp % 2) * 1280
                    def f(eng):
                        r = None
                        t = 0
                        for hh in range(2):
                            poff = prot + hh * 640
                            cols = [ki * 128 for ki in range(nkt)] + [512]
                            for cc in cols:
                                r = eng.transpose(PSb[:, pdst + t * 128:pdst + (t + 1) * 128],
                                                  Pb[:, poff + cc:poff + cc + 128], ident[:])
                                t += 1
                        return r
                    return f
                M[("ptp", qt, hp)] = S.e("tensor", pt_fn(qt, hp), ("sPE",), banks={3 * rot, 3 * rot + 1, 3 * rot + 2})

                S.w("vector", "sPE", M[("ptp", qt, hp)]["sPE"])
                ntile = 2 * (nkt + 1)

                def ptev_fn(qtt, hpp):
                    prot = (hpp % 2) * 1280
                    def f(eng):
                        return eng.tensor_copy(PTb[:, prot:prot + ntile * 128], PSb[:, pdst:pdst + ntile * 128])
                    return f
                M[("ptev", qt, hp)] = S.e("vector", ptev_fn(qt, hp), ("sDVE",), banks={3 * rot, 3 * rot + 1, 3 * rot + 2})

                if upto < 6:
                    continue
                S.w("tensor", "sDVE", M[("ptev", qt, hp)]["sDVE"])
                S.w("tensor", "sACT", M[("vev", qt)]["sACT"])
                if hp >= 2:
                    S.w("tensor", "sDVE", M[("otev", qt, hp - 2)]["sDVE"])
                elif qt > QT0:
                    S.w("tensor", "sDVE", M[("otev", qt - 1, 6 + hp)]["sDVE"])

                def av_fn(qtt, hpp):
                    prot = (hpp % 2) * 1280
                    ocol = 3072 + (hpp % 2) * 512
                    def f(eng):
                        r = None
                        nmm = nkt + 1
                        for hh in range(2):
                            h = 2 * hpp + hh
                            for ki in range(nmm):
                                ka = kts[ki] if ki < nkt else qtt
                                slot = ka % 8
                                lhs = v_ring[:, slot * 1024 + h * 64: slot * 1024 + h * 64 + 64]
                                rhs = PTb[:, prot + (hh * nmm + ki) * 128: prot + (hh * nmm + ki) * 128 + 128]
                                r = eng.matmul(PSf[0:64, ocol + hh * 128: ocol + hh * 128 + 128],
                                               lhs, rhs,
                                               start=(hh == 0 and ki == 0),
                                               stop=(hh == 1 and ki == nmm - 1))
                        return r
                    return f
                M[("av", qt, hp)] = S.e("tensor", av_fn(qt, hp), ("sPE",), banks={6 + rot})

                S.w("vector", "sPE", M[("av", qt, hp)]["sPE"])

                def otev_fn(qtt, hpp):
                    ocol = 3072 + (hpp % 2) * 512
                    def f(eng):
                        return eng.tensor_copy(oT[0:64, hpp * 256:hpp * 256 + 256],
                                               PSf[0:64, ocol:ocol + 256])
                    return f
                M[("otev", qt, hp)] = S.e("vector", otev_fn(qt, hp), ("sDVE",), banks={6 + rot})

            if upto < 7 or upto in (11, 13, 15):
                return
            S.w("tensor", "sDVE", M[("otev", qt, 7)]["sDVE"])
            S.w("tensor", "sACT", M[("exp", qt, 6)]["sACT"])

            def wo_fn(qtt):
                def f(eng):
                    r = None
                    for h in range(16):
                        lhs = oT[0:64, h * 128:(h + 1) * 128]
                        for half in range(2):
                            rhs = wout_b[0:64, h * 1024 + half * 512:h * 1024 + half * 512 + 512]
                            r = eng.matmul(PSf[:, half * 512:half * 512 + 512], lhs, rhs,
                                           start=(h == 0), stop=(h == 15))
                    return r
                return f
            M[("wo", qt)] = S.e("tensor", wo_fn(qt), ("sPE",), banks={0, 1})

            S.w("scalar", "sPE", M[("wo", qt)]["sPE"])
            M[("ssq2", qt)] = S.e("scalar",
                                  lambda eng: eng.activation(junk[:], PSf[:, 0:1024], AF.Square,
                                                             accum_out=st2[:, 1:2]),
                                  ("sACT",), banks={0, 1})
            S.w("vector", "sPE", M[("wo", qt)]["sPE"])
            S.w("vector", "sACT", M[("ssq2", qt)]["sACT"])
            pre2 = S.chain("vector", banks={0, 1}, fns=[
                lambda eng: eng.reduce_sum(st2[:, 0:1], PSf[:, 0:1024], axis=mybir.AxisListType.X),
                lambda eng: eng.tensor_scalar(st2[:, 2:3], st2[:, 0:1], 1.0 / 1024.0, None, ALU.mult),
                lambda eng: eng.tensor_tensor(st2[:, 3:4], st2[:, 2:3], st2[:, 2:3], ALU.mult),
            ])
            S.w("vector", "sDVE", pre2["sDVE"])
            M[("var2", qt)] = S.e("vector",
                                  lambda eng: eng.scalar_tensor_tensor(st2[:, 4:5], st2[:, 1:2], 1.0 / 1024.0,
                                                                       st2[:, 3:4], ALU.mult, ALU.subtract),
                                  ("sDVE",))
            S.w("scalar", "sDVE", M[("var2", qt)]["sDVE"])
            M[("rs2", qt)] = S.chain("scalar", [
                lambda eng: eng.activation(st2[:, 5:6], st2[:, 4:5], AF.Ln, bias=CE, scale=1.0),
                lambda eng: eng.activation(st2[:, 6:7], st2[:, 5:6], AF.Exp, bias=0.0, scale=CH),
            ])
            S.w("vector", "sACT", M[("rs2", qt)]["sACT"])
            M[("nb2", qt)] = S.e("vector",
                                 lambda eng: eng.scalar_tensor_tensor(st2[:, 7:8], st2[:, 2:3], -1.0,
                                                                      st2[:, 6:7], ALU.mult, ALU.mult),
                                 ("sDVE",))
            S.w("scalar", "sDVE", M[("nb2", qt)]["sDVE"])
            if qt > QT0:
                S.w("scalar", "sPOOL", M[("x2", qt - 1)]["sPOOL"])
            M[("t1", qt)] = S.e("scalar",
                                lambda eng: eng.activation(t1b[:], PSf[:, 0:1024], AF.Identity,
                                                           bias=st2[:, 7:8], scale=st2[:, 6:7]),
                                ("sACT",), banks={0, 1})

            S.w("gpsimd", "sACT", M[("t1", qt)]["sACT"])
            if qt >= QT0 + 2:
                psem = "sSPA" if (qt - 2) % 2 == 0 else "sSPB"
                S.w("gpsimd", psem, M[("spill", qt - 2)][psem])

            def x2_fn(qtt):
                def f(eng):
                    return eng.tensor_tensor(x2s[:, (qtt % 2) * 1024:(qtt % 2) * 1024 + 1024],
                                             x_tile(qtt), t1b[:], ALU.add)
                return f
            M[("x2", qt)] = S.e("gpsimd", x2_fn(qt), ("sPOOL",))
            S.w("sync", "sPOOL", M[("x2", qt)]["sPOOL"])
            spsem = "sSPA" if qt % 2 == 0 else "sSPB"
            M[("spill", qt)] = S.e("sync",
                                   (lambda qtt: (lambda eng: eng.dma_start(
                                       out=d_spill[(qtt - 3) * 128:(qtt - 2) * 128, :],
                                       in_=x2s[:, (qtt % 2) * 1024:(qtt % 2) * 1024 + 1024])))(qt),
                                   (spsem,), dma=True)

        emit_phase1()

        p1 = {s: S.c(s) for s in ("sPE", "sACT", "sDVE", "sPOOL")}

        def run_block():
            with nc.Block() as block:
                def runner(eng_obj, lst):
                    for item in lst:
                        if item[0] == "w":
                            eng_obj.wait_ge(sems[item[1]], item[2])
                        else:
                            inst = item[1](eng_obj)
                            for sem, amt in item[2]:
                                inst.then_inc(sems[sem], amt)

                @block.sync
                def _(e):
                    runner(e, S.ops["sync"])

                @block.tensor
                def _(e):
                    runner(e, S.ops["tensor"])

                @block.vector
                def _(e):
                    runner(e, S.ops["vector"])

                @block.scalar
                def _(e):
                    runner(e, S.ops["scalar"])

                @block.gpsimd
                def _(e):
                    runner(e, S.ops["gpsimd"])
            for k in ENGS:
                S.ops[k] = []

        run_block()
        es1.close()

        es2 = ExitStack()

        def sb2(name, shape, dt):
            return es2.enter_context(nc.sbuf_tensor(name, shape, dt))

        wr_b = sb2("wr_b", [128, 8 * 1024], BF16)
        wk_b = sb2("wk_b", [128, 8 * 1024], BF16)
        wv_b = sb2("wv_b", [128, 8 * 1024], BF16)
        stag2 = sb2("stag2", [128, 2 * 1024], F32)
        x2t = sb2("x2t", [128, 5 * 1024], F32)
        xl2 = sb2("xl2", [128, 2 * 1024], BF16)
        x2T = sb2("x2T", [128, 8 * 640], BF16)
        dbuf = sb2("dbuf", [128, 8 * 512], BF16)
        xrT = sb2("xrT", [128, 8 * 512], BF16)
        xkT = sb2("xkT", [128, 8 * 512], BF16)
        rbuf = sb2("rbuf", [128, 2 * 1024], F32)
        kbuf = sb2("kbuf", [128, 2 * 1024], BF16)
        kact = sb2("kact", [128, 2 * 1024], BF16)
        kactT = sb2("kactT", [128, 8 * 512], BF16)
        sgt = sb2("sgt", [128, 2 * 1024], F32)
        ffn = sb2("ffn", [128, 2 * 1024], F32)
        t2b = sb2("t2b", [128, 2 * 1024], F32)
        outb = sb2("outb", [128, 2 * 1024], F32)
        junk2 = sb2("junk2", [128, 1024], BF16)
        st3 = sb2("st3", [128, 16], F32)
        st4 = sb2("st4", [128, 16], F32)

        PS2 = es2.enter_context(nc.psum_tensor("PS2", [128, 4096], F32))
        PF = PS2.ap()
        PB2 = PS2.ap().bitcast(BF16)

        for eng in ENGS:
            for s in ("sPE", "sACT", "sDVE", "sPOOL"):
                S.w(eng, s, p1[s])

        for i in range(24):
            w = (d_wr, d_wk, d_wv)[i // 8]
            dstb = (wr_b, wk_b, wv_b)[i // 8]
            c = i % 8
            src = w[c * 128:(c + 1) * 128, :]
            dst = dstb[:, c * 1024:(c + 1) * 1024]
            slot = stag2[:, (i % 2) * 1024:(i % 2) * 1024 + 1024]
            wsem = "sWA" if i % 2 == 0 else "sWB"
            if i >= 2:
                S.w("sync", "sDVE", M[("cast2", i - 2)]["sDVE"])
            M[("wld2", i)] = S.e("sync",
                                 (lambda ss, hh: (lambda eng: eng.dma_start(out=ss, in_=hh)))(slot, src),
                                 (wsem,), dma=True)
            S.w("vector", wsem, M[("wld2", i)][wsem])
            M[("cast2", i)] = S.e("vector",
                                  (lambda ss, dd: (lambda eng: eng.tensor_copy(dd, ss)))(slot, dst),
                                  ("sDVE",))

        def st3c(tt, k):
            base = (tt % 2) * 8
            return st3[:, base + k:base + k + 1]

        def st4c(ot, k):
            base = (ot % 2) * 8
            return st4[:, base + k:base + k + 1]

        def emit_phase2():
            for sc in range(4):
                for tt in range(5):
                    gt = 4 * sc + tt
                    if sc > 0:
                        S.w("sync", "sDVE", M[("c2", sc - 1, tt)]["sDVE"])
                        if tt >= 1:
                            S.w("sync", "sPOOL", M[("ob", sc - 1, tt - 1)]["sPOOL"])
                    spsem = "sSPA" if (gt + 3) % 2 == 0 else "sSPB"
                    S.w("sync", spsem, M[("spill", gt + 3)][spsem])
                    if ("x2ld_prev",) in M:
                        S.w("sync", "sX2", M[("x2ld_prev",)]["sX2"])
                    M[("x2ld", sc, tt)] = S.e("sync",
                                              (lambda g, t: (lambda eng: eng.dma_start(
                                                  out=x2t[:, t * 1024:(t + 1) * 1024],
                                                  in_=d_spill[g * 128:(g + 1) * 128, :])))(gt, tt),
                                              ("sX2",), dma=True)
                    M[("x2ld_prev",)] = M[("x2ld", sc, tt)]
                    S.w("scalar", "sX2", M[("x2ld", sc, tt)]["sX2"])
                    M[("ssq3", sc, tt)] = S.e("scalar",
                                              (lambda t: (lambda eng: eng.activation(junk2[:], x2t[:, t * 1024:(t + 1) * 1024],
                                                                                     AF.Square, accum_out=st3c(t, 1))))(tt),
                                              ("sACT",))
                    S.w("vector", "sX2", M[("x2ld", sc, tt)]["sX2"])
                    pre3 = S.chain("vector", [
                        (lambda t: (lambda eng: eng.reduce_sum(st3c(t, 0), x2t[:, t * 1024:(t + 1) * 1024],
                                                               axis=mybir.AxisListType.X)))(tt),
                        (lambda t: (lambda eng: eng.tensor_scalar(st3c(t, 2), st3c(t, 0), 1.0 / 1024.0, None, ALU.mult)))(tt),
                        (lambda t: (lambda eng: eng.tensor_tensor(st3c(t, 3), st3c(t, 2), st3c(t, 2), ALU.mult)))(tt),
                    ])
                    S.w("vector", "sACT", M[("ssq3", sc, tt)]["sACT"])
                    S.w("vector", "sDVE", pre3["sDVE"])
                    M[("var3", sc, tt)] = S.e("vector",
                                              (lambda t: (lambda eng: eng.scalar_tensor_tensor(
                                                  st3c(t, 4), st3c(t, 1), 1.0 / 1024.0, st3c(t, 3),
                                                  ALU.mult, ALU.subtract)))(tt),
                                              ("sDVE",))
                    S.w("scalar", "sDVE", M[("var3", sc, tt)]["sDVE"])
                    M[("rs3", sc, tt)] = S.chain("scalar", [
                        (lambda t: (lambda eng: eng.activation(st3c(t, 5), st3c(t, 4), AF.Ln, bias=CE, scale=1.0)))(tt),
                        (lambda t: (lambda eng: eng.activation(st3c(t, 6), st3c(t, 5), AF.Exp, bias=0.0, scale=CH)))(tt),
                    ])
                    S.w("vector", "sACT", M[("rs3", sc, tt)]["sACT"])
                    M[("c2", sc, tt)] = S.chain("vector", [
                        (lambda t: (lambda eng: eng.scalar_tensor_tensor(st3c(t, 7), st3c(t, 2), -1.0, st3c(t, 6),
                                                                         ALU.mult, ALU.mult)))(tt),
                        (lambda t: (lambda eng: eng.tensor_scalar(xl2[:, (t % 2) * 1024:(t % 2) * 1024 + 1024],
                                                                  x2t[:, t * 1024:(t + 1) * 1024],
                                                                  st3c(t, 6), st3c(t, 7), ALU.mult, ALU.add)))(tt),
                    ])

                    S.w("tensor", "sDVE", M[("c2", sc, tt)]["sDVE"])
                    if (sc, tt) != (0, 0):
                        prev = (sc, tt - 1) if tt > 0 else (sc - 1, 4)
                        S.w("tensor", "sDVE", M[("e2", prev[0], prev[1])]["sDVE"])
                    if sc > 0 and tt == 0:
                        S.w("tensor", "sDVE", M[("ke", sc - 1, 3)]["sDVE"])

                    def t2_fn(t):
                        def f(eng):
                            r = None
                            for c in range(8):
                                r = eng.transpose(PB2[:, 6144 + c * 128:6272 + c * 128],
                                                  xl2[:, (t % 2) * 1024 + c * 128:(t % 2) * 1024 + (c + 1) * 128],
                                                  ident[:])
                            return r
                        return f
                    M[("pt2", sc, tt)] = S.e("tensor", t2_fn(tt), ("sPE",), banks={6})
                    S.w("vector", "sPE", M[("pt2", sc, tt)]["sPE"])

                    def e2_fn(t):
                        dst = bass.AP(x2T, t * 128, [[5120, 128], [640, 8], [1, 128]])

                        def f(eng):
                            return eng.tensor_copy(dst, PB2[:, 6144:7168].rearrange("p (c f) -> p c f", c=8))
                        return f
                    M[("e2", sc, tt)] = S.e("vector", e2_fn(tt), ("sDVE",), banks={6})

                if sc > 0:
                    S.w("vector", "sPE", M[("kmm", sc - 1, 3)]["sPE"])
                S.w("vector", "sDVE", M[("e2", sc, 4)]["sDVE"])

                def d_fn(eng):
                    src1 = bass.AP(x2T, 128, [[5120, 128], [640, 8], [1, 512]])
                    src0 = bass.AP(x2T, 127, [[5120, 128], [640, 8], [1, 512]])
                    dv = bass.AP(dbuf, 0, [[4096, 128], [512, 8], [1, 512]])
                    return eng.tensor_tensor(dv, src1, src0, ALU.subtract)

                def mix_fn(eng):
                    r = None
                    for c in range(8):
                        xp = bass.AP(x2T, c * 640 + 127, [[5120, 128], [1, 512]])
                        eng.scalar_tensor_tensor(xrT[:, c * 512:(c + 1) * 512],
                                                 dbuf[:, c * 512:(c + 1) * 512],
                                                 mus[:, c:c + 1], xp, ALU.mult, ALU.add)
                        r = eng.scalar_tensor_tensor(xkT[:, c * 512:(c + 1) * 512],
                                                     dbuf[:, c * 512:(c + 1) * 512],
                                                     mus[:, 8 + c:9 + c], xp, ALU.mult, ALU.add)
                    return r
                M[("mix", sc)] = S.chain("vector", [d_fn, mix_fn])

                for ot in range(4):
                    gt = 4 * sc + ot + 1
                    S.w("tensor", "sDVE", M[("mix", sc)]["sDVE"])
                    if sc + ot > 0:
                        prev = (sc, ot - 1) if ot > 0 else (sc - 1, 3)
                        S.w("tensor", "sACT", M[("relu", prev[0], prev[1])]["sACT"])
                    if ot == 0 and sc == 0:
                        S.w("tensor", "sDVE", M[("cast2", 23)]["sDVE"])

                    def rmm_fn(o):
                        def f(eng):
                            r = None
                            for c in range(8):
                                lhs = xrT[:, c * 512 + o * 128: c * 512 + o * 128 + 128]
                                for half in range(2):
                                    r = eng.matmul(PF[:, half * 512:half * 512 + 512], lhs,
                                                   wr_b[:, c * 1024 + half * 512: c * 1024 + half * 512 + 512],
                                                   start=(c == 0), stop=(c == 7))
                            return r
                        return f
                    M[("rmm", sc, ot)] = S.e("tensor", rmm_fn(ot), ("sPE",), banks={0, 1})

                    def kmm_fn(o):
                        def f(eng):
                            r = None
                            for c in range(8):
                                lhs = xkT[:, c * 512 + o * 128: c * 512 + o * 128 + 128]
                                for half in range(2):
                                    r = eng.matmul(PF[:, 1024 + half * 512:1024 + half * 512 + 512], lhs,
                                                   wk_b[:, c * 1024 + half * 512: c * 1024 + half * 512 + 512],
                                                   start=(c == 0), stop=(c == 7))
                            return r
                        return f
                    M[("kmm", sc, ot)] = S.e("tensor", kmm_fn(ot), ("sPE",), banks={2, 3})

                    S.w("scalar", "sPE", M[("rmm", sc, ot)]["sPE"])

                    def sigE_fn(o):
                        def f(eng):
                            return eng.activation(sgt[:, (o % 2) * 1024:(o % 2) * 1024 + 1024],
                                                  PF[:, 0:1024], AF.Exp, bias=0.0, scale=CN)
                        return f
                    M[("sigE", sc, ot)] = S.e("scalar", sigE_fn(ot), ("sACT",), banks={0, 1})
                    S.w("scalar", "sPE", M[("kmm", sc, ot)]["sPE"])

                    def relu_fn(o):
                        def f(eng):
                            return eng.activation(kbuf[:, (o % 2) * 1024:(o % 2) * 1024 + 1024],
                                                  PF[:, 1024:2048], AF.Relu)
                        return f
                    M[("relu", sc, ot)] = S.e("scalar", relu_fn(ot), ("sACT",), banks={2, 3})

                    S.w("vector", "sACT", M[("sigE", sc, ot)]["sACT"])

                    M[("sig", sc, ot)] = S.chain("vector", [
                        (lambda o: (lambda eng: eng.tensor_scalar(sgt[:, (o % 2) * 1024:(o % 2) * 1024 + 1024],
                                                                  sgt[:, (o % 2) * 1024:(o % 2) * 1024 + 1024],
                                                                  1.0, None, ALU.add)))(ot),
                        (lambda o: (lambda eng: eng.reciprocal(rbuf[:, (o % 2) * 1024:(o % 2) * 1024 + 1024],
                                                               sgt[:, (o % 2) * 1024:(o % 2) * 1024 + 1024])))(ot),
                    ])
                    S.w("vector", "sACT", M[("relu", sc, ot)]["sACT"])

                    def ka_fn(o):
                        def f(eng):
                            so = (o % 2) * 1024
                            return eng.tensor_tensor(kact[:, so:so + 1024], kbuf[:, so:so + 1024],
                                                     kbuf[:, so:so + 1024], ALU.mult)
                        return f
                    M[("ka", sc, ot)] = S.e("vector", ka_fn(ot), ("sDVE",))

                    S.w("tensor", "sDVE", M[("ka", sc, ot)]["sDVE"])
                    if sc + ot > 0:
                        prev = (sc, ot - 1) if ot > 0 else (sc - 1, 3)
                        S.w("tensor", "sDVE", M[("ke", prev[0], prev[1])]["sDVE"])

                    def tk2_fn(o):
                        def f(eng):
                            r = None
                            so = (o % 2) * 1024
                            for c in range(8):
                                r = eng.transpose(PB2[:, 7168 + c * 128:7296 + c * 128],
                                                  kact[:, so + c * 128: so + (c + 1) * 128], ident[:])
                            return r
                        return f
                    M[("ptk2", sc, ot)] = S.e("tensor", tk2_fn(ot), ("sPE",), banks={7})
                    S.w("vector", "sPE", M[("ptk2", sc, ot)]["sPE"])

                    def ke_fn(o):
                        dst = bass.AP(kactT, o * 128, [[4096, 128], [512, 8], [1, 128]])

                        def f(eng):
                            return eng.tensor_copy(dst, PB2[:, 7168:8192].rearrange("p (c f) -> p c f", c=8))
                        return f
                    M[("ke", sc, ot)] = S.e("vector", ke_fn(ot), ("sDVE",), banks={7})

                    S.w("tensor", "sDVE", M[("ke", sc, ot)]["sDVE"])
                    if sc + ot > 0:
                        prev = (sc, ot - 1) if ot > 0 else (sc - 1, 3)
                        S.w("tensor", "sDVE", M[("ff", prev[0], prev[1])]["sDVE"])

                    def vmm_fn(o):
                        def f(eng):
                            r = None
                            for c in range(8):
                                lhs = kactT[:, c * 512 + o * 128: c * 512 + o * 128 + 128]
                                for half in range(2):
                                    r = eng.matmul(PF[:, 2048 + half * 512:2048 + half * 512 + 512], lhs,
                                                   wv_b[:, c * 1024 + half * 512: c * 1024 + half * 512 + 512],
                                                   start=(c == 0), stop=(c == 7))
                            return r
                        return f
                    M[("vmm", sc, ot)] = S.e("tensor", vmm_fn(ot), ("sPE",), banks={4, 5})

                    S.w("vector", "sPE", M[("vmm", sc, ot)]["sPE"])
                    if sc * 4 + ot >= 2:
                        pr = (sc, ot - 2) if ot >= 2 else (sc - 1, 2 + ot)
                        S.w("vector", "sACT", M[("t2", pr[0], pr[1])]["sACT"])

                    S.w("vector", "sDVE", M[("sig", sc, ot)]["sDVE"])

                    def ff_fn(o):
                        def f(eng):
                            so = (o % 2) * 1024
                            return eng.tensor_tensor(ffn[:, so:so + 1024], PF[:, 2048:3072],
                                                     rbuf[:, so:so + 1024], ALU.mult)
                        return f
                    M[("ff", sc, ot)] = S.e("vector", ff_fn(ot), ("sDVE",), banks={4, 5})

                    S.w("scalar", "sDVE", M[("ff", sc, ot)]["sDVE"])

                    def ssq4_fn(o):
                        def f(eng):
                            so = (o % 2) * 1024
                            return eng.activation(junk2[:], ffn[:, so:so + 1024], AF.Square,
                                                  accum_out=st4c(o, 1))
                        return f
                    M[("ssq4", sc, ot)] = S.e("scalar", ssq4_fn(ot), ("sACT",))
                    S.w("vector", "sDVE", M[("ff", sc, ot)]["sDVE"])
                    pre4 = S.chain("vector", [
                        (lambda o: (lambda eng: eng.reduce_sum(st4c(o, 0),
                                                               ffn[:, (o % 2) * 1024:(o % 2) * 1024 + 1024],
                                                               axis=mybir.AxisListType.X)))(ot),
                        (lambda o: (lambda eng: eng.tensor_scalar(st4c(o, 2), st4c(o, 0), 1.0 / 1024.0, None, ALU.mult)))(ot),
                        (lambda o: (lambda eng: eng.tensor_tensor(st4c(o, 3), st4c(o, 2), st4c(o, 2), ALU.mult)))(ot),
                    ])
                    S.w("vector", "sACT", M[("ssq4", sc, ot)]["sACT"])
                    S.w("vector", "sDVE", pre4["sDVE"])
                    M[("var4", sc, ot)] = S.e("vector",
                                              (lambda o: (lambda eng: eng.scalar_tensor_tensor(
                                                  st4c(o, 4), st4c(o, 1), 1.0 / 1024.0, st4c(o, 3),
                                                  ALU.mult, ALU.subtract)))(ot),
                                              ("sDVE",))
                    S.w("scalar", "sDVE", M[("var4", sc, ot)]["sDVE"])
                    M[("rs4", sc, ot)] = S.chain("scalar", [
                        (lambda o: (lambda eng: eng.activation(st4c(o, 5), st4c(o, 4), AF.Ln, bias=CE, scale=1.0)))(ot),
                        (lambda o: (lambda eng: eng.activation(st4c(o, 6), st4c(o, 5), AF.Exp, bias=0.0, scale=CH)))(ot),
                    ])
                    S.w("vector", "sACT", M[("rs4", sc, ot)]["sACT"])
                    M[("nb4", sc, ot)] = S.e("vector",
                                             (lambda o: (lambda eng: eng.scalar_tensor_tensor(
                                                 st4c(o, 7), st4c(o, 2), -1.0, st4c(o, 6), ALU.mult, ALU.mult)))(ot),
                                             ("sDVE",))
                    S.w("scalar", "sDVE", M[("nb4", sc, ot)]["sDVE"])
                    if sc * 4 + ot >= 2:
                        pr = (sc, ot - 2) if ot >= 2 else (sc - 1, 2 + ot)
                        S.w("scalar", "sPOOL", M[("ob", pr[0], pr[1])]["sPOOL"])

                    def t2o_fn(o):
                        def f(eng):
                            so = (o % 2) * 1024
                            return eng.activation(t2b[:, so:so + 1024], ffn[:, so:so + 1024], AF.Identity,
                                                  bias=st4c(o, 7), scale=st4c(o, 6))
                        return f
                    M[("t2", sc, ot)] = S.e("scalar", t2o_fn(ot), ("sACT",))

                    S.w("gpsimd", "sACT", M[("t2", sc, ot)]["sACT"])
                    if sc * 4 + ot >= 2:
                        pr = (sc, ot - 2) if ot >= 2 else (sc - 1, 2 + ot)
                        stsem = "sSTA" if pr[1] % 2 == 0 else "sSTB"
                        S.w("gpsimd", stsem, M[("ost", pr[0], pr[1])][stsem])

                    def ob_fn(o):
                        def f(eng):
                            so = (o % 2) * 1024
                            return eng.tensor_tensor(outb[:, so:so + 1024],
                                                     x2t[:, (o + 1) * 1024:(o + 2) * 1024],
                                                     t2b[:, so:so + 1024], ALU.add)
                        return f
                    M[("ob", sc, ot)] = S.e("gpsimd", ob_fn(ot), ("sPOOL",))
                    S.w("sync", "sPOOL", M[("ob", sc, ot)]["sPOOL"])
                    grow = sc * 512 + ot * 128
                    stsem = "sSTA" if ot % 2 == 0 else "sSTB"
                    M[("ost", sc, ot)] = S.e("sync",
                                             (lambda g, o: (lambda eng: eng.dma_start(
                                                 out=d_out[g:g + 128, :],
                                                 in_=outb[:, (o % 2) * 1024:(o % 2) * 1024 + 1024])))(grow, ot),
                                             (stsem,), dma=True)

            S.w("sync", "sSTA", S.c("sSTA"))
            S.w("sync", "sSTB", S.c("sSTB"))

        if do_p2 and upto >= 99:
            emit_phase2()
        else:
            for sq in ("sSPA", "sSPB"):
                S.w("sync", sq, S.c(sq))
        run_block()
        es2.close()

        nc.all_engine_barrier()
        nc.gpsimd.dma_reset(sem_rng)
        nc.gpsimd.sem_clear(sem_rng)
        nc.all_engine_barrier()

    build_nc.last_sched = S
    return nc



def _prep_shared(inputs):
    w_qkv = np.ascontiguousarray(inputs["L0_attn_w_qkv"], dtype=np.float32)
    perm = np.zeros(D, dtype=np.int64)
    for h in range(NH):
        base = h * HD
        perm[base:base + 32] = base + np.arange(0, HD, 2)
        perm[base + 32:base + 64] = base + np.arange(1, HD, 2)
    wq = w_qkv[:, 0:D][:, perm]
    wk = w_qkv[:, D:2 * D][:, perm]
    wv = w_qkv[:, 2 * D:3 * D]
    wqkv_p = np.ascontiguousarray(np.concatenate([wq, wk, wv], axis=1))
    wo_raw = inputs["L0_attn_w_out"].astype(np.float32)
    wout = np.ascontiguousarray(wo_raw.reshape(16, 64, D).transpose(1, 0, 2).reshape(64, 16 * D))
    wr = np.ascontiguousarray(inputs["L0_ffn_W_r"].T, dtype=np.float32)
    wkf = np.ascontiguousarray(inputs["L0_ffn_W_k"].T, dtype=np.float32)
    wvf = np.ascontiguousarray(inputs["L0_ffn_W_v"].T, dtype=np.float32)
    mus = np.zeros((128, 16), np.float32)
    mus[:, 0:8] = inputs["L0_ffn_mu_r"].astype(np.float32).reshape(8, 128).T
    mus[:, 8:16] = inputs["L0_ffn_mu_k"].astype(np.float32).reshape(8, 128).T
    consts = np.zeros((128, 8), np.float32)
    consts[:, 0] = -BEXP
    consts[:, 1] = 0.125
    consts[:, 2] = EPS
    consts[:, 3] = -0.5
    consts[:, 4] = -1.0
    ident = np.eye(128, dtype=bfloat16)
    ii = np.arange(128)
    mfirst = np.where(ii[None, :] >= ii[:, None] + 1, 0.0, MASKV).astype(bfloat16)
    mlast = np.where(ii[None, :] <= ii[:, None], 0.0, MASKV).astype(bfloat16)
    return dict(wqkv=wqkv_p, wout=wout, wr=wr, wk=wkf, wv=wvf, mus=mus,
                consts=consts, ident=ident, mfirst=mfirst, mlast=mlast)


def _tabs_for(pos):
    p = np.maximum(pos, 0).astype(np.float64)
    inv_freq = 1.0 / 10000.0 ** (np.arange(0, HD, 2, dtype=np.float64) / HD)
    fr = p[:, None] * inv_freq[None, :]
    cos = np.cos(fr)
    sin = np.sin(fr)
    t = np.zeros((EXT, 128), np.float32)
    t[:, 0:32] = cos
    t[:, 32:64] = cos
    t[:, 64:96] = -sin
    t[:, 96:128] = sin
    return t


def _tabs_sbufmajor(t):
    out = np.zeros((128, EXT), np.float32)
    for j in range(NT):
        out[:, j * 128:(j + 1) * 128] = t[j * 128:(j + 1) * 128, :]
    return out


def _mq_for(half):
    ii = np.arange(128)
    mfirst = np.where(ii[None, :] >= ii[:, None] + 1, 0.0, MASKV).astype(np.float32)
    out = np.zeros((128, 2048), np.float32)
    for q in range(4):
        qt = 4 + q
        blk = np.zeros((128, 512), np.float32)
        if half == 1:
            blk[:, 0:128] = mfirst
        else:
            for ki in range(4):
                ka = qt - 4 + ki
                if ka < 4:
                    blk[:, ki * 128:(ki + 1) * 128] = MASKV
        out[:, q * 512:(q + 1) * 512] = blk
    return out.astype(bfloat16)


def kernel(L0_attn_x, L0_attn_w_qkv, L0_attn_w_out, L0_ffn_W_r, L0_ffn_W_k,
           L0_ffn_W_v, L0_ffn_mu_r, L0_ffn_mu_k):
    inputs = dict(L0_attn_x=np.asarray(L0_attn_x), L0_attn_w_qkv=np.asarray(L0_attn_w_qkv),
                  L0_attn_w_out=np.asarray(L0_attn_w_out), L0_ffn_W_r=np.asarray(L0_ffn_W_r),
                  L0_ffn_W_k=np.asarray(L0_ffn_W_k), L0_ffn_W_v=np.asarray(L0_ffn_W_v),
                  L0_ffn_mu_r=np.asarray(L0_ffn_mu_r), L0_ffn_mu_k=np.asarray(L0_ffn_mu_k))
    sh = _prep_shared(inputs)
    x = inputs["L0_attn_x"].astype(np.float32)

    tabs_e = _tabs_sbufmajor(_tabs_for(np.arange(-512, 2048)))
    tabs_o = _tabs_sbufmajor(_tabs_for(np.arange(1536, 4096)))
    mq_e = _mq_for(0)
    mq_o = _mq_for(1)

    in_maps = []
    for core in range(8):
        b, half = core // 2, core % 2
        if half == 0:
            x_ext = np.concatenate([np.zeros((512, D), np.float32), x[b, 0:2048]], axis=0)
        else:
            x_ext = np.ascontiguousarray(x[b, 1536:4096])
        in_maps.append({
            "x_ext": x_ext,
            "wqkv": sh["wqkv"], "wout": sh["wout"],
            "wr": sh["wr"], "wk": sh["wk"], "wv": sh["wv"],
            "tabs": (tabs_o if half else tabs_e),
            "mus": sh["mus"], "consts": sh["consts"],
            "ident": sh["ident"],
            "mfirst": sh["mfirst"], "mlast": sh["mlast"],
            "mq": (mq_o if half else mq_e),
        })

    nc = build_nc()
    res = run_bass_kernel_spmd(nc, in_maps, core_ids=list(range(8)))
    out = np.zeros((4, 4096, D), np.float32)
    for core in range(8):
        b, half = core // 2, core % 2
        out[b, half * 2048:(half + 1) * 2048] = res.results[core]["out"]
    return out
